# version 21
# baseline (speedup 1.0000x reference)
import numpy as np
import ml_dtypes

import concourse.bass as bass
import concourse.mybir as mybir
from concourse.bass_utils import run_bass_kernel_spmd
from contextlib import ExitStack

F32 = mybir.dt.float32
BF16 = mybir.dt.bfloat16
I32 = mybir.dt.int32
AX = mybir.AxisListType.X
ALU = mybir.AluOpType
ACTF = mybir.ActivationFunctionType

MAGIC = 256.0

ENGS = ("sy", "te", "ve", "se", "ge")


class CFG:
    T = 2048
    D = 1024
    INNER = 4096
    H = 16
    HD = 64
    WINDOW = 512
    TT = 256
    QUARTER = 512
    RING = 1024
    N_CORES = 8
    slopes = [2.0 ** (-(i + 1) / 2.0) for i in range(16)]


def _ap_ranges(fn):
    out = []
    for v in (fn.__defaults__ or ()):
        try:
            t = v.tensor
            name = t.name
            ap = v.ap
            off = int(v.offset)
            span = 1
            dims = list(ap)
            for d in dims[1:]:
                span += int(d[0]) * (int(d[1]) - 1)
            out.append((name, off, off + span))
        except Exception:
            try:
                out.append((v.tensor.name, 0, 1 << 60))
            except Exception:
                pass
    return out


def _overlap(a, b):
    return a[0] == b[0] and a[1] < b[2] and b[1] < a[2]


class B:
    def __init__(self, nc, stack):
        self.nc = nc
        self.stack = stack
        self.rec = {e: [] for e in ENGS}
        self.ch = {}
        for e in ENGS:
            self.chan(e)
        self.lastw = {}
        self.pend = {e: [] for e in ENGS}

    def chan(self, name):
        if name not in self.ch:
            self.ch[name] = [self.stack.enter_context(
                self.nc.semaphore(f"c_{name}")), 0]
        return name

    def op(self, eng, fn, inc=True, amt=1, nodep=False):
        if eng in ("ve", "se", "ge") and not nodep:
            rng = _ap_ranges(fn)
            pend = self.pend[eng]
            if not rng or any(_overlap(a, pb) for a in rng for pb in pend):
                self.wait(eng, eng)
                pend.clear()
            pend.extend(rng)
        if inc:
            name = eng if inc is True else self.chan(inc)
            sem = self.ch[name][0]
            self.ch[name][1] += amt
            self.rec[eng].append(lambda e, fn=fn, sem=sem, amt=amt:
                                 fn(e).then_inc(sem, amt))
        else:
            self.rec[eng].append(fn)

    def dma(self, eng, out_ap, in_ap, slow=False, ch=None):
        self.op(eng, lambda e, o=out_ap, i=in_ap, s=slow: e.dma_start(
            out=o, in_=i, allow_slow_non_contiguous=s), amt=16,
            inc=(True if ch is None else ch))

    def now(self, name):
        return self.ch[name][1]

    def wait(self, eng, prod, val=None):
        self.chan(prod)
        if val is None:
            val = self.ch[prod][1]
        if val <= 0 or self.lastw.get((eng, prod), 0) >= val:
            return
        self.lastw[(eng, prod)] = val
        sem = self.ch[prod][0]
        self.rec[eng].append(lambda e, sem=sem, v=val: e.wait_ge(sem, v))

    def selfwait(self, eng):
        self.wait(eng, eng)
        if eng in self.pend:
            self.pend[eng].clear()

    def barrier(self):
        for e in ENGS:
            for p in self.ch:
                if p != e:
                    self.wait(e, p)


def build_core(cfg: CFG):
    nc = bass.Bass("TRN2", target_bir_lowering=False)
    stack = ExitStack()
    c = cfg
    n_tc = c.T // 128

    xin = nc.dram_tensor("x", [c.T, c.D], F32, kind="ExternalInput").ap()
    w_q = nc.dram_tensor("wq", [c.D, c.D], F32, kind="ExternalInput").ap()
    w_k = nc.dram_tensor("wk", [c.D, c.D], F32, kind="ExternalInput").ap()
    w_v = nc.dram_tensor("wv", [c.D, c.D], F32, kind="ExternalInput").ap()
    w_o = nc.dram_tensor("wo", [c.D, c.D], F32, kind="ExternalInput").ap()
    qnw_in = nc.dram_tensor("qnw", [c.D], F32, kind="ExternalInput").ap()
    knw_in = nc.dram_tensor("knw", [c.D], F32, kind="ExternalInput").ap()
    w_g = nc.dram_tensor("wg", [c.INNER, c.D], F32, kind="ExternalInput").ap()
    w_u = nc.dram_tensor("wu", [c.INNER, c.D], F32, kind="ExternalInput").ap()
    w_d = nc.dram_tensor("wd", [c.D, c.INNER], F32, kind="ExternalInput").ap()
    id32_in = nc.dram_tensor("id32", [128, 128], F32, kind="ExternalInput").ap()
    idbf_in = nc.dram_tensor("idbf", [128, 128], BF16, kind="ExternalInput").ap()
    dist_in = nc.dram_tensor("dist_m", [128, 640], F32, kind="ExternalInput").ap()
    yout = nc.dram_tensor("y", [c.T, c.D], F32, kind="ExternalOutput").ap()

    wgT_d = nc.dram_tensor("wgT", [c.D, c.INNER], BF16).ap()
    wuT_d = nc.dram_tensor("wuT", [c.D, c.INNER], BF16).ap()
    wdT_d = nc.dram_tensor("wdT", [c.INNER, c.D], BF16).ap()
    wqT_d = nc.dram_tensor("wqTs", [c.D, c.D], F32).ap()
    wkT_d = nc.dram_tensor("wkTs", [c.D, c.D], F32).ap()
    wvT_d = nc.dram_tensor("wvTs", [c.D, c.D], F32).ap()
    woT_d = nc.dram_tensor("woTs", [c.D, c.D], F32).ap()

    x_res = stack.enter_context(nc.sbuf_tensor("x_res", [128, n_tc * c.D], F32))
    id32 = stack.enter_context(nc.sbuf_tensor("id32s", [128, 128], F32))
    idbf = stack.enter_context(nc.sbuf_tensor("idbfs", [128, 128], BF16))
    qnw = stack.enter_context(nc.sbuf_tensor("qnw_s", [128, 8], F32))
    knw = stack.enter_context(nc.sbuf_tensor("knw_s", [128, 8], F32))
    wsc = stack.enter_context(nc.sbuf_tensor("wsc", [128, 8], F32))
    ones4 = stack.enter_context(nc.sbuf_tensor("ones4", [4, 128], F32))
    cpool = stack.enter_context(nc.sbuf_tensor("cpool", [128, 4], F32))
    ln = {}
    for k in ("rsum", "ssq", "m", "ex2", "m2", "var", "std", "rstd", "mrstd"):
        ln[k] = stack.enter_context(nc.sbuf_tensor(f"ln_{k}", [128, 1], F32))[:]
    ln["bn6"] = stack.enter_context(nc.sbuf_tensor("ln_bn6", [128, 12], F32))[:]
    ln["mv"] = stack.enter_context(nc.sbuf_tensor("ln_mv", [128, 2], F32))[:]

    b = B(nc, stack)

    b.dma("sy", id32[:], id32_in[:, :], ch="ini")
    b.dma("sy", idbf[:], idbf_in[:, :], ch="ini")
    b.dma("sy", qnw[:], qnw_in.rearrange("(c p) -> p c", p=128), slow=True, ch="ini")
    b.dma("sy", knw[:], knw_in.rearrange("(c p) -> p c", p=128), slow=True, ch="ini")
    for tc in range(n_tc):
        b.dma("ge", x_res[:, tc * c.D:(tc + 1) * c.D], xin[tc * 128:(tc + 1) * 128, :],
              ch="xin")
    b.wait("te", "ini")
    b.wait("ve", "ini")
    b.op("ve", lambda e: e.memset(ones4[:], 1.0))
    b.op("ve", lambda e: e.memset(cpool[:, 0:1], 1e-5))
    b.op("ve", lambda e: e.memset(cpool[:, 1:2], 1e-6))
    c_eps_ln = cpool[:, 0:1]
    c_eps_rms = cpool[:, 1:2]

    with (
        nc.sbuf_tensor("pw_t0", [128, 4096], F32) as pw_t0,
        nc.sbuf_tensor("pw_t1", [128, 4096], F32) as pw_t1,
        nc.sbuf_tensor("pw_sums", [128, 28], F32) as pw_sums,
        nc.sbuf_tensor("pw_tot", [4, 4], F32) as pw_tot,
        nc.sbuf_tensor("pw_diag", [4, 8], F32) as pw_diag,
        nc.psum_tensor("pw_ps", [128, 512], F32) as pw_ps,
    ):
        pw_t = [pw_t0, pw_t1]
        b.op("ve", lambda e: e.memset(pw_sums[:, 24:28], 0.0))
        slabs = []
        for widx, wap in enumerate((w_g, w_u)):
            wr = wap.rearrange("(s q p) n -> s p q n", p=128, q=4)
            for s in range(8):
                slabs.append((widx, s, wr[s]))
        for s in range(8):
            slabs.append((2, s, w_d[s * 128:(s + 1) * 128, :]
                          .rearrange("p (q n) -> p q n", q=4)))
        cap_red = {}
        cap_dma = {}
        for i, (widx, s, src) in enumerate(slabs):
            sl = i % 2
            if i >= 2:
                b.wait("sy", "ve", cap_red.get(i - 2, 0))
            b.dma("sy", pw_t[sl][:].rearrange("p (q n) -> p q n", q=4), src,
                  ch=f"pwl{sl}")
            cap_dma[i] = b.now(f"pwl{sl}")
            b.wait("ve", f"pwl{sl}", cap_dma[i])
            b.op("ve", lambda e, o=pw_sums[:, widx * 8 + s:widx * 8 + s + 1],
                 i=pw_t[sl][:]:
                 e.reduce_sum(o, i, AX, apply_absolute_value=True))
            cap_red[i] = b.now("ve")
        for widx in range(3):
            b.op("ve", lambda e, o=pw_sums[:, 24 + widx:25 + widx],
                 i=pw_sums[:, widx * 8:widx * 8 + 8]:
                 e.reduce_sum(o, i, AX))
        b.wait("te", "ve")
        b.op("te", lambda e: e.matmul(pw_ps[0:4, 0:128], pw_sums[:, 24:28],
                                      id32[:, :], start=True, stop=True,
                                      is_transpose=True))
        b.wait("ve", "te")
        b.op("ve", lambda e: e.reduce_sum(pw_tot[:, 0:1], pw_ps[0:4, 0:128], AX))
        b.op("ve", lambda e: e.tensor_scalar(pw_tot[:, 1:2], pw_tot[:, 0:1],
                                             1.0 / (c.INNER * c.D), 1e-5,
                                             ALU.mult, ALU.max))
        b.op("ve", lambda e: e.reciprocal(pw_tot[:, 2:3], pw_tot[:, 1:2]))
        b.op("ve", lambda e: e.tensor_tensor(pw_diag[:, 0:4], id32[0:4, 0:4],
                                             pw_tot[:, 1:2].broadcast_to((4, 4)),
                                             ALU.mult))
        b.op("ve", lambda e: e.tensor_tensor(pw_diag[:, 4:8], id32[0:4, 0:4],
                                             pw_tot[:, 2:3].broadcast_to((4, 4)),
                                             ALU.mult))
        b.wait("te", "ve")
        b.op("te", lambda e: e.matmul(pw_ps[:, 0:8], ones4[:, :], pw_diag[:, 0:8],
                                      start=True, stop=True))
        b.wait("ve", "te")
        b.op("ve", lambda e: e.tensor_copy(wsc[:, 0:4], pw_ps[:, 0:4]))
        b.op("ve", lambda e: e.tensor_copy(wsc[:, 4:8], pw_ps[:, 4:8]))
        b.barrier()

    def tern_transpose(wap, wT_dst, widx, rows, cols):
        nrb = rows // 128
        nch = cols // 512
        nm = f"tt{widx}"
        sw_ap = wsc[:, 4 + widx:5 + widx]
        nring = 4 if nrb > 8 else 2
        spb = nrb // 4
        with (
            nc.sbuf_tensor(nm + "_w", [128, nring * 2048], F32) as tt_w,
            nc.sbuf_tensor(nm + "_t", [128, nrb * 512], BF16) as tt_t,
            nc.sbuf_tensor(nm + "_abs", [128, 4096], F32) as tt_abs,
            nc.sbuf_tensor(nm + "_sgn", [128, 4096], F32) as tt_sgn,
            nc.sbuf_tensor(nm + "_asm", [128, 2 * rows], BF16) as tt_asm,
            nc.psum_tensor(nm + "_ps", [128, 2048], BF16) as tt_ps,
        ):
            cap_half_T = 0
            cap_wb = {}
            cap_cp = {}
            cap_seq = {}
            cap_vq = {}
            gg_ctr = [0]
            for h in range(nch):
                first_t = [True]
                for s in range(spb):
                    gs = h * spb + s
                    b.wait("sy", "se", cap_seq.get(gs - 2, 0))
                    slot = (gs % nring) * 2048
                    src = wap.rearrange("(r q p) n -> r p q n", p=128, q=4)[
                        s][:, :, h * 512:(h + 1) * 512]
                    chn = f"ttw{gs % 2}"
                    b.dma("sy", tt_w[:, slot:slot + 2048]
                          .rearrange("p (q n) -> p q n", q=4), src, ch=chn)
                    dcap = b.now(chn)
                    par = (gs % 2) * 2048
                    b.wait("se", chn, dcap)
                    b.wait("se", "ve", cap_vq.get(gs - 2, 0))
                    b.op("se", lambda e, o=tt_abs[:, par:par + 2048],
                         i=tt_w[:, slot:slot + 2048]:
                         e.activation(o, i, ACTF.Abs))
                    b.op("se", lambda e, o=tt_sgn[:, par:par + 2048],
                         i=tt_w[:, slot:slot + 2048]:
                         e.activation(o, i, ACTF.Sign))
                    cap_seq[gs] = b.now("se")
                    b.wait("ve", "se", cap_seq[gs])
                    b.op("ve", lambda e, o=tt_abs[:, par:par + 2048],
                         i=tt_abs[:, par:par + 2048], s_=sw_ap:
                         e.tensor_scalar(o, i, s_, 0.5, ALU.mult, ALU.is_ge))
                    if first_t[0]:
                        b.wait("ve", "te", cap_half_T)
                        first_t[0] = False
                    b.op("ve", lambda e, o=tt_t[:, s * 2048:(s + 1) * 2048],
                         a=tt_abs[:, par:par + 2048], s_=tt_sgn[:, par:par + 2048]:
                         e.tensor_tensor(o, a, s_, ALU.mult))
                    cap_vq[gs] = b.now("ve")
                cap_half_quant = cap_vq[h * spb + spb - 1]
                for cg in range(4):
                    gc = h * 4 + cg
                    s2 = gc % 2
                    for rb8 in range(0, nrb, 8):
                        gg = gg_ctr[0]
                        gg_ctr[0] += 1
                        pp = (gg % 2) * 1024
                        b.wait("te", "ve",
                               max(cap_half_quant, cap_cp.get(gg - 2, 0)))
                        for j in range(8):
                            rb = rb8 + j
                            src = tt_t[:, rb * 512 + cg * 128:rb * 512 + (cg + 1) * 128]
                            b.op("te", lambda e, o=tt_ps[:, pp + j * 128:pp + (j + 1) * 128],
                                 s=src, st=(j == 0), sp=(j == 7):
                                 e.matmul(o, s, idbf[:, :], start=st, stop=sp,
                                          is_transpose=True), inc=(j == 7))
                        b.wait("ve", "te")
                        if rb8 == 0:
                            b.wait("ve", f"wb{s2}", cap_wb.get(gc - 2, 0))
                        b.op("ve", lambda e,
                             o=tt_asm[:, s2 * rows + rb8 * 128:s2 * rows + (rb8 + 8) * 128],
                             i=tt_ps[:, pp:pp + 1024]: e.tensor_copy(o, i))
                        cap_cp[gg] = b.now("ve")
                    b.selfwait("ve")
                    b.wait("ge", "ve")
                    row = h * 512 + cg * 128
                    b.dma("ge", wT_dst[row:row + 128, :],
                          tt_asm[:, s2 * rows:(s2 + 1) * rows], ch=f"wb{s2}")
                    cap_wb[gc] = b.now(f"wb{s2}")
                cap_half_T = b.now("te")
                cap_half_vq = cap_vq[nrb // 4 - 1]
            b.barrier()

    tern_transpose(w_g, wgT_d, 0, c.INNER, c.D)
    tern_transpose(w_u, wuT_d, 1, c.INNER, c.D)

    with (
        nc.sbuf_tensor("at_w0", [128, 8 * 1024], F32) as at_w0,
        nc.sbuf_tensor("at_w1", [128, 8 * 1024], F32) as at_w1,
        nc.sbuf_tensor("at_asm", [128, 2048], F32) as at_asm,
        nc.psum_tensor("at_ps", [128, 1024], F32) as at_ps,
    ):
        at_w = [at_w0, at_w1]
        cap_at_te = [0, 0]
        cap_aw = {}
        cap_acp = {}
        agg = [0]
        for wi, (wap, wT_dst) in enumerate(((w_q, wqT_d), (w_k, wkT_d),
                                            (w_v, wvT_d), (w_o, woT_d))):
            sl = wi % 2
            b.wait("sy", "te", cap_at_te[sl])
            for rb in range(8):
                b.dma("sy", at_w[sl][:, rb * 1024:(rb + 1) * 1024],
                      wap[rb * 128:(rb + 1) * 128, :], ch=f"awl{sl}")
            dcap = b.now(f"awl{sl}")
            b.wait("te", f"awl{sl}", dcap)
            for r in range(8):
                ar = wi * 8 + r
                s2 = ar % 2
                for g4 in range(2):
                    gg = agg[0]
                    agg[0] += 1
                    pp = (gg % 2) * 512
                    b.wait("te", "ve", cap_acp.get(gg - 2, 0))
                    for j in range(4):
                        rb = g4 * 4 + j
                        src = at_w[sl][:, rb * 1024 + r * 128:rb * 1024 + (r + 1) * 128]
                        b.op("te", lambda e, o=at_ps[:, pp + j * 128:pp + (j + 1) * 128],
                             s=src, st=(j == 0), sp=(j == 3):
                             e.matmul(o, s, id32[:, :], start=st, stop=sp,
                                      is_transpose=True), inc=(j == 3))
                    b.wait("ve", "te")
                    if g4 == 0:
                        b.wait("ve", f"aw{s2}", cap_aw.get(ar - 2, 0))
                    b.op("ve", lambda e,
                         o=at_asm[:, s2 * 1024 + g4 * 512:s2 * 1024 + (g4 + 1) * 512],
                         i=at_ps[:, pp:pp + 512]: e.tensor_copy(o, i))
                    cap_acp[gg] = b.now("ve")
                b.selfwait("ve")
                b.wait("ge", "ve")
                b.dma("ge", wT_dst[r * 128:(r + 1) * 128, :],
                      at_asm[:, s2 * 1024:(s2 + 1) * 1024], ch=f"aw{s2}")
                cap_aw[ar] = b.now(f"aw{s2}")
            cap_at_te[sl] = b.now("te")
        tern_transpose(w_d, wdT_d, 2, c.D, c.INNER)
        b.barrier()

    def layer_norm(tc, dst_ap):
        src = x_res[:, tc * c.D:(tc + 1) * c.D]
        b.op("ve", lambda e, o=ln["bn6"][:, 0:6], i=src[:, 0:512]: e.bn_stats(o, i))
        b.op("ve", lambda e, o=ln["bn6"][:, 6:12], i=src[:, 512:1024]: e.bn_stats(o, i))
        b.op("ve", lambda e, o=ln["mv"], i=ln["bn6"]: e.bn_aggr(o, i))
        b.wait("se", "ve")
        b.op("se", lambda e, o=ln["std"], i=ln["mv"][:, 1:2]:
             e.activation(o, i, ACTF.Sqrt, bias=c_eps_ln, scale=1.0))
        b.wait("ve", "se")
        b.op("ve", lambda e, o=ln["rstd"], i=ln["std"]: e.reciprocal(o, i))
        b.op("ve", lambda e, o=ln["mrstd"], a=ln["mv"][:, 0:1], x=ln["rstd"]:
             e.tensor_tensor(o, a, x, ALU.mult))
        b.op("ve", lambda e, o=dst_ap, i=src, s=ln["rstd"], mb=ln["mrstd"]:
             e.scalar_tensor_tensor(o, i, s, mb.broadcast_to((128, c.D)),
                                    ALU.mult, ALU.subtract))

    MAGICRND = 12582912.0

    def quant_chain(src_ap, s_ap, dst_ap, y2=None, e2=None, q2i=None):
        b.op("ve", lambda e, o=src_ap, i=src_ap, s=s_ap:
             e.tensor_scalar(o, i, s, MAGICRND, ALU.mult, ALU.add))
        b.op("ve", lambda e, o=dst_ap, i=src_ap:
             e.tensor_scalar(o, i, MAGICRND, None, ALU.subtract))

    _f_ctr = [0]

    def ffn_call():
        ntt = c.T // c.TT
        ctc = c.TT // 128
        nm = f"f{_f_ctr[0]}"; _f_ctr[0] += 1
        with ExitStack() as fstk:
            E = fstk.enter_context
            f_xln = E(nc.sbuf_tensor(nm + "_xln", [128, c.D], F32))
            f_xqT = E(nc.sbuf_tensor(nm + "_xqT", [128, 8 * c.TT], BF16))
            f_g = E(nc.sbuf_tensor(nm + "_g", [128, ctc * c.INNER], F32))
            f_gq = E(nc.sbuf_tensor(nm + "_gq", [128, 2048], BF16))
            f_gqb = [f_gq, f_gq]
            f_gqT = E(nc.sbuf_tensor(nm + "_gqT", [128, 16 * c.TT], BF16))
            f_wg0 = E(nc.sbuf_tensor(nm + "_wg0", [128, 8192], BF16))
            f_wg1 = E(nc.sbuf_tensor(nm + "_wg1", [128, 8192], BF16))
            f_wu0 = E(nc.sbuf_tensor(nm + "_wu0", [128, 8192], BF16))
            f_wu1 = E(nc.sbuf_tensor(nm + "_wu1", [128, 8192], BF16))
            f_wdr = E(nc.sbuf_tensor(nm + "_wdr", [128, 4096], BF16))
            f_sg = E(nc.sbuf_tensor(nm + "_sg", [128, 2048], F32))
            f_amax = E(nc.sbuf_tensor(nm + "_amax", [128, 1], F32))
            f_sc = E(nc.sbuf_tensor(nm + "_sc", [128, 2], F32))
            f_oscs = E(nc.sbuf_tensor(nm + "_oscs", [128, 3 * n_tc], F32))
            f_ps = [E(nc.psum_tensor(nm + f"_ps{i}", [128, 512], F32))
                    for i in range(8)]
            f_pst_gu = f_ps[0][:].bitcast(BF16)
            f_pst_gq = f_ps[2][:].bitcast(BF16)
            f_wd = [f_wdr[:, i * 1024:(i + 1) * 1024] for i in range(4)]
            f_wgb = [f_wg0, f_wg1]
            f_wub = [f_wu0, f_wu1]

            def load_gu(pr):
                osl2 = pr % 4
                sl = pr % 2
                chn = f"wgl{sl}"
                b.dma("sy", f_wgb[sl][:].rearrange("p (ic w) -> p ic w", ic=8),
                      wgT_d.rearrange("(ic p) n -> p ic n", p=128)
                      [:, :, osl2 * 1024:(osl2 + 1) * 1024], ch=chn)
                b.dma("sy", f_wub[sl][:].rearrange("p (ic w) -> p ic w", ic=8),
                      wuT_d.rearrange("(ic p) n -> p ic n", p=128)
                      [:, :, osl2 * 1024:(osl2 + 1) * 1024], ch=chn)
                return (chn, b.now(chn))

            f_xqT1 = E(nc.sbuf_tensor(nm + "_xqT1", [128, 8 * c.TT], BF16))
            f_sq = E(nc.sbuf_tensor(nm + "_sq2", [128, 2], F32))
            f_gmax = E(nc.sbuf_tensor(nm + "_gmax", [128, 16], F32))
            f_xqTb = [f_xqT, f_xqT1]
            cap_lnqT = [0, 0]

            def lnq_compute(tt, j):
                tc = tt * ctc + j
                layer_norm(tc, f_xln[:])
                b.op("ve", lambda e, o=f_amax[:], i=f_xln[:]:
                     e.reduce_max(o, i, AX, apply_absolute_value=True))
                b.op("ve", lambda e, o=f_sc[:, 1:2], i=f_amax[:]:
                     e.tensor_scalar(o, i, 1e-5, 1.0 / 127.0, ALU.max, ALU.mult))
                b.op("ve", lambda e, o=f_sc[:, 0:1], i=f_sc[:, 1:2]:
                     e.reciprocal(o, i))
                b.op("ve", lambda e, o=f_oscs[:, 3 * tc:3 * tc + 1],
                     a=f_sc[:, 1:2], x=wsc[:, 0:1]:
                     e.tensor_tensor(o, a, x, ALU.mult))
                b.op("ve", lambda e, o=f_oscs[:, 3 * tc + 1:3 * tc + 2],
                     a=f_sc[:, 1:2], x=wsc[:, 1:2]:
                     e.tensor_tensor(o, a, x, ALU.mult))
                b.wait("ve", "te")
                quant_chain(f_xln[:], f_sc[:, 0:1],
                            f_gq[:, j * 1024:(j + 1) * 1024])
                return b.now("ve")

            def lnq_transpose(tt, j, pst, te_free=(0, 0), ve_ready=0):
                dst_x = f_xqTb[tt % 2]
                gqx = f_gq[:, j * 1024:(j + 1) * 1024]
                b.wait("te", "ve", ve_ready)
                b.wait("te", "ve", te_free[0])
                b.wait("te", "se", te_free[1])
                for ic in range(8):
                    b.op("te", lambda e, o=pst[:, ic * 128:(ic + 1) * 128],
                         s=gqx[:, ic * 128:(ic + 1) * 128],
                         st=(ic == 0), sp=(ic == 7):
                         e.matmul(o, s, idbf[:, :], start=st, stop=sp,
                                  is_transpose=True), inc=(ic == 7))
                tcap = b.now("te")
                b.wait("ve", "te", tcap)
                b.op("ve", lambda e,
                     o=dst_x[:].rearrange("p (ic t) -> p ic t", ic=8)
                     [:, :, j * 128:(j + 1) * 128],
                     i=pst[:, 0:1024].rearrange("p (ic t) -> p ic t", ic=8):
                     e.tensor_copy(o, i))
                return b.now("ve")

            def lnq_chunk(tt, j, pst=None, te_free=(0, 0)):
                if pst is None:
                    pst = f_pst_gu
                vr = lnq_compute(tt, j)
                return lnq_transpose(tt, j, pst, te_free, vr)

            def lnq_tile(tt):
                for j in range(ctc):
                    lnq_chunk(tt, j)

            lnq_tile(0)
            for tt in range(ntt):
                xq_cur = f_xqTb[tt % 2]
                b.wait("te", "ve")
                b.wait("te", "se")
                cap_wdma = {}
                cap_evac = {}
                cap_mm = {}
                cap_wdma[0] = load_gu(0)
                cap_wdma[1] = load_gu(1)
                gp = 0
                lnq_caps = [0, 0]
                for pr in range(4):
                    osl_a, osl_b = 2 * pr, 2 * pr + 1
                    wsl = pr % 2
                    for j in range(ctc):
                        tc = tt * ctc + j
                        grp = (gp % 2) * 4
                        tiles = f_ps[grp:grp + 4]
                        if pr == 3 and j == 0 and tt + 1 < ntt:
                            lcap = lnq_transpose(tt + 1, 0, f_pst_gu,
                                                 cap_evac[gp - 2], lnq_caps[0])
                            cap_evac[gp - 2] = (max(cap_evac[gp - 2][0], lcap),
                                                cap_evac[gp - 2][1])
                        b.wait("te", cap_wdma[pr][0], cap_wdma[pr][1])
                        if gp >= 2:
                            b.wait("te", "ve", cap_evac[gp - 2][0])
                            b.wait("te", "se", cap_evac[gp - 2][1])
                        for ic in range(8):
                            lhs = xq_cur[:].rearrange("p (ic t) -> p ic t", ic=8)[
                                :, ic, j * 128:(j + 1) * 128]
                            for q in range(4):
                                buf = f_wgb[wsl] if q % 2 == 0 else f_wub[wsl]
                                half = q // 2
                                b.op("te", lambda e, o=tiles[q][:], l=lhs,
                                     r=buf[:, ic * 1024 + half * 512:
                                           ic * 1024 + (half + 1) * 512],
                                     st=(ic == 0), sp=(ic == 7):
                                     e.matmul(o, l, r, start=st, stop=sp),
                                     inc=(ic == 7 and q == 3))
                        cap_mm[gp] = b.now("te")
                        if j == ctc - 1 and pr + 2 < 4:
                            b.wait("sy", "te", cap_mm[gp])
                            cap_wdma[pr + 2] = load_gu(pr + 2)
                        for sub in range(2):
                            osl = osl_a if sub == 0 else osl_b
                            pg, pu = tiles[2 * sub], tiles[2 * sub + 1]
                            b.wait("se", "te", cap_mm[gp])
                            sgo = (gp % 2) * 1024 + sub * 512
                            b.op("se", lambda e, o=f_sg[:, sgo:sgo + 512], i=pg[:],
                                 s=f_oscs[:, 3 * tc:3 * tc + 1]:
                                 e.activation(o, i, ACTF.Silu, scale=s))
                            secap = b.now("se")
                            b.wait("ve", "se", secap)
                            b.op("ve", lambda e,
                                 o=f_g[:, j * c.INNER + osl * 512:
                                       j * c.INNER + (osl + 1) * 512],
                                 i=pu[:], s=f_oscs[:, 3 * tc + 1:3 * tc + 2],
                                 x=f_sg[:, sgo:sgo + 512]:
                                 e.scalar_tensor_tensor(o, i, s, x, ALU.mult,
                                                        ALU.mult))
                            b.op("ve", lambda e,
                                 o=f_gmax[:, osl * 2 + j:osl * 2 + j + 1],
                                 i=f_g[:, j * c.INNER + osl * 512:
                                        j * c.INNER + (osl + 1) * 512]:
                                 e.reduce_max(o, i, AX, apply_absolute_value=True))
                        cap_evac[gp] = (b.now("ve"), b.now("se"))
                        gp += 1
                    if pr == 1 and tt + 1 < ntt:
                        lnq_caps[0] = lnq_compute(tt + 1, 0)
                    if pr == 2 and tt + 1 < ntt:
                        lnq_caps[1] = lnq_compute(tt + 1, 1)

                for j in range(ctc):
                    tc = tt * ctc + j
                    b.op("ve", lambda e, o=f_amax[:],
                         i=f_gmax[:, j:j + 16:2] if False else
                         f_gmax[:].rearrange("p (o j) -> p j o", j=2)[:, j, :]:
                         e.reduce_max(o, i, AX))
                    b.op("ve", lambda e, o=f_sc[:, 1:2], i=f_amax[:]:
                         e.tensor_scalar(o, i, 1e-5, 1.0 / 127.0, ALU.max, ALU.mult))
                    b.op("ve", lambda e, o=f_sq[:, j:j + 1], i=f_sc[:, 1:2]:
                         e.reciprocal(o, i))
                    b.op("ve", lambda e, o=f_oscs[:, 3 * tc + 2:3 * tc + 3],
                         a=f_sc[:, 1:2], x=wsc[:, 2:3]:
                         e.tensor_tensor(o, a, x, ALU.mult))
                    b.op("ve", lambda e, o=f_oscs[:, 3 * tc + 2:3 * tc + 3],
                         i=f_oscs[:, 3 * tc + 2:3 * tc + 3]:
                         e.tensor_scalar(o, i, 0.5, None, ALU.mult))

                cap_d = {}
                for jc0 in range(4):
                    b.dma("sy", f_wd[jc0][:], wdT_d[jc0 * 128:(jc0 + 1) * 128, :],
                          ch=f"wdl{jc0}")
                    cap_d[jc0] = b.now(f"wdl{jc0}")
                cap_dmm = {}
                caps_gqT_ev = {}
                if tt + 1 < ntt:
                    lnq_transpose(tt + 1, 1, f_pst_gq,
                                  (b.now("ve"), b.now("se")), lnq_caps[1])
                b.wait("ve", "te")
                last_T = [0]
                for sl in range(4):
                    for j in range(ctc):
                        gj = f_g[:, j * c.INNER:(j + 1) * c.INNER]
                        b.wait("ve", "te", last_T[0])
                        slq = (sl % 2) * 1024
                        quant_chain(gj[:, sl * 1024:(sl + 1) * 1024],
                                    f_sq[:, j:j + 1],
                                    f_gqb[j][:, slq:slq + 1024])
                        b.wait("te", "ve")
                        b.wait("te", "se")
                        for k in range(8):
                            b.op("te", lambda e, o=f_pst_gq[:, k * 128:(k + 1) * 128],
                                 s=f_gqb[j][:, slq + k * 128:slq + (k + 1) * 128],
                                 st=(k == 0), sp=(k == 7):
                                 e.matmul(o, s, idbf[:, :], start=st, stop=sp,
                                          is_transpose=True), inc=(k == 7))
                        tcap = b.now("te")
                        last_T[0] = tcap
                        b.wait("se", "te", tcap)
                        b.op("se", lambda e,
                             o=f_gqT[:].rearrange("p (jc t) -> p jc t", jc=16)
                             [:, (sl % 2) * 8:(sl % 2 + 1) * 8, j * 128:(j + 1) * 128],
                             i=f_pst_gq[:, 0:1024].rearrange("p (k t) -> p k t", k=8):
                             e.copy(o, i))
                    caps_gqT_ev[sl] = b.now("se")
                    b.wait("te", "se", caps_gqT_ev[sl])
                    if sl == 0:
                        b.wait("te", "ve")
                    for k in range(8):
                        jc = sl * 8 + k
                        slot = jc % 4
                        b.wait("te", f"wdl{slot}", cap_d[jc])
                        for j in range(ctc):
                            lhs = f_gqT[:].rearrange("p (q t) -> p q t", q=16)[
                                :, jc % 16, j * 128:(j + 1) * 128]
                            for hh in range(2):
                                if j == 0:
                                    dstp = (f_ps[6] if hh == 0 else f_ps[7])[:]
                                else:
                                    dstp = (f_ps[0] if hh == 0 else f_ps[1])[:]
                                b.op("te", lambda e, o=dstp, l=lhs,
                                     r=f_wd[slot][:, hh * 512:(hh + 1) * 512],
                                     st=(jc == 0), sp=(jc == 31):
                                     e.matmul(o, l, r, start=st, stop=sp),
                                     inc=(j == ctc - 1 and hh == 1))
                        cap_dmm[jc] = b.now("te")
                        if jc + 4 < 32:
                            b.wait("sy", "te", cap_dmm[jc])
                            b.dma("sy", f_wd[slot][:],
                                  wdT_d[(jc + 4) * 128:(jc + 5) * 128, :],
                                  ch=f"wdl{slot}")
                            cap_d[jc + 4] = b.now(f"wdl{slot}")
                b.wait("ve", "te")
                for j2 in range(2):
                    tcx = tt * ctc + j2
                    pair = (f_ps[6], f_ps[7]) if j2 == 0 else (f_ps[0], f_ps[1])
                    for hh in range(2):
                        b.op("ve", lambda e,
                             o=x_res[:, tcx * c.D + hh * 512:
                                     tcx * c.D + (hh + 1) * 512],
                             i=pair[hh][:, 0:512],
                             s=f_oscs[:, 3 * tcx + 2:3 * tcx + 3],
                             x=x_res[:, tcx * c.D + hh * 512:
                                     tcx * c.D + (hh + 1) * 512]:
                             e.scalar_tensor_tensor(o, i, s, x, ALU.mult, ALU.add))
            b.barrier()

    _a_ctr = [0]

    def attn_call():
        nq = c.T // c.QUARTER
        cq = c.QUARTER // 128
        nm = f"a{_a_ctr[0]}"; _a_ctr[0] += 1
        with ExitStack() as astk:
            E = astk.enter_context
            a_xln = E(nc.sbuf_tensor(nm + "_xln", [128, c.D], F32))
            a_xlnT = E(nc.sbuf_tensor(nm + "_xlnT", [128, 8 * c.QUARTER], F32))
            dist_m = E(nc.sbuf_tensor(nm + "_dist", [128, 640], F32))
            b.dma("sy", dist_m[:], dist_in[:, :], ch="adist")
            b.wait("ve", "adist")
            a_qnT = E(nc.sbuf_tensor(nm + "_qnT", [128, 8 * c.QUARTER], F32))
            a_knr = E(nc.sbuf_tensor(nm + "_knr", [128, 8 * c.RING], F32))
            a_vr = E(nc.sbuf_tensor(nm + "_vr", [128, 8 * c.D], F32))
            a_w0 = E(nc.sbuf_tensor(nm + "_w0", [128, c.D], F32))
            a_w1 = E(nc.sbuf_tensor(nm + "_w1", [128, c.D], F32))
            a_pn00 = E(nc.sbuf_tensor(nm + "_pn00", [128, 640], F32))
            a_pn01 = E(nc.sbuf_tensor(nm + "_pn01", [128, 640], F32))
            a_pn10 = E(nc.sbuf_tensor(nm + "_pn10", [128, 640], F32))
            a_pn11 = E(nc.sbuf_tensor(nm + "_pn11", [128, 640], F32))
            a_pt0 = E(nc.sbuf_tensor(nm + "_pt0", [128, 640], F32))
            a_pt1 = E(nc.sbuf_tensor(nm + "_pt1", [128, 640], F32))
            a_aT4 = E(nc.sbuf_tensor(nm + "_aT4", [128, cq * 1024], F32))
            a_st = E(nc.sbuf_tensor(nm + "_st", [128, 8], F32))
            ap0 = E(nc.psum_tensor(nm + "_ap0", [128, 1024], F32))
            ap1 = E(nc.psum_tensor(nm + "_ap1", [128, 1024], F32))
            ap2 = E(nc.psum_tensor(nm + "_ap2", [128, 1024], F32))
            ap3 = E(nc.psum_tensor(nm + "_ap3", [128, 1024], F32))
            a_pn = [[a_pn00, a_pn01], [a_pn10, a_pn11]]
            a_pt = [a_pt0, a_pt1]
            a_w = [a_w0, a_w1]
            ap = [ap0, ap1, ap2, ap3]
            scps = [ap2, ap3]

            def seglist(kbeg, kend):
                segs = []
                kk = kbeg
                while kk < kend:
                    off = kk - kbeg
                    col = kk % c.RING
                    take = min(kend - kk, c.RING - col,
                               512 - (off % 512) if off % 512 else 512, 512)
                    segs.append((off, col, take))
                    kk += take
                return segs

            def emit_scores(h, j, q0q, par):
                q0 = q0q + j * 128
                kbeg = max(0, q0 - c.WINDOW)
                kend = q0 + 128
                nk = kend - kbeg
                segs = seglist(kbeg, kend)
                bank_first = {}
                bank_last = {}
                for s_ in segs:
                    bk = s_[0] // 512
                    bank_first.setdefault(bk, s_)
                    bank_last[bk] = s_
                hp = 64 * (h % 2)
                ht = h // 2
                pss = scps[par]
                for s_ in segs:
                    (off, col, take) = s_
                    lhs = a_qnT[hp:hp + 64,
                                ht * c.QUARTER + j * 128:ht * c.QUARTER + (j + 1) * 128]
                    rhs = a_knr[hp:hp + 64,
                                ht * c.RING + col:ht * c.RING + col + take]
                    b.op("te", lambda e, o=pss[:, off:off + take], l=lhs, r=rhs,
                         st=(bank_first[off // 512] == s_),
                         sp=(bank_last[off // 512] == s_):
                         e.matmul(o, l, r, start=st, stop=sp),
                         inc=(s_ == segs[-1]))
                return nk

            cap_sc = {}
            cap_stt = {}
            cap_exp = {}
            cap_pn = {}
            cap_ptsb = {}
            cap_ptT = {}
            cap_avp = {}
            cap_aTe = {}
            for qq in range(nq):
                q0q = qq * c.QUARTER
                for j in range(cq):
                    tc = qq * cq + j
                    b.wait("ve", "te")
                    layer_norm(tc, a_xln[:])
                    b.wait("te", "ve")
                    for g4 in range(2):
                        ps = scps[g4]
                        for ic4 in range(4):
                            ic = g4 * 4 + ic4
                            b.op("te", lambda e, o=ps[:, ic4 * 128:(ic4 + 1) * 128],
                                 s=a_xln[:, ic * 128:(ic + 1) * 128],
                                 st=(ic4 == 0), sp=(ic4 == 3):
                                 e.matmul(o, s, id32[:, :], start=st, stop=sp,
                                          is_transpose=True), inc=(ic4 == 3))
                    b.wait("se", "te")
                    for g4 in range(2):
                        ps = scps[g4]
                        b.op("se", lambda e,
                             o=a_xlnT[:].rearrange("p (ic t) -> p ic t", ic=8)
                             [:, g4 * 4:(g4 + 1) * 4, j * 128:(j + 1) * 128],
                             i=ps[:, 0:512].rearrange("p (ic t) -> p ic t", ic=4):
                             e.copy(o, i))

                cap_evj = {j_: (b.now("ve"), b.now("se")) for j_ in range(cq)}
                for proj in ("k", "v", "q"):
                    wT_src = {"q": wqT_d, "k": wkT_d, "v": wvT_d}[proj]
                    cap_w = {}
                    b.wait("sy", "te")
                    b.dma("sy", a_w[0][:], wT_src[0:128, :], ch="awl0")
                    cap_w[0] = b.now("awl0")
                    b.dma("sy", a_w[1][:], wT_src[128:256, :], ch="awl1")
                    cap_w[1] = b.now("awl1")
                    for ic in range(8):
                        b.wait("te", f"awl{ic % 2}", cap_w[ic])
                        for j in range(cq):
                            if ic == 0 and j in cap_evj:
                                b.wait("te", "ve", cap_evj[j][0])
                                b.wait("te", "se", cap_evj[j][1])
                            lhs = a_xlnT[:].rearrange("p (ic t) -> p ic t", ic=8)[
                                :, ic, j * 128:(j + 1) * 128]
                            for hh in range(2):
                                b.op("te", lambda e,
                                     o=ap[j][:, hh * 512:(hh + 1) * 512], l=lhs,
                                     r=a_w[ic % 2][:, hh * 512:(hh + 1) * 512],
                                     st=(ic == 0), sp=(ic == 7):
                                     e.matmul(o, l, r, start=st, stop=sp),
                                     inc=(j == cq - 1 and hh == 1))
                        if ic + 2 < 8:
                            b.wait("sy", "te")
                            b.dma("sy", a_w[ic % 2][:],
                                  wT_src[(ic + 2) * 128:(ic + 3) * 128, :],
                                  ch=f"awl{ic % 2}")
                            cap_w[ic + 2] = b.now(f"awl{ic % 2}")
                    cap_proj = b.now("te")
                    for j in range(cq):
                        if proj == "v":
                            vslot = (qq * cq + j) % 8
                            b.wait("ve", "te", cap_proj)
                            b.op("ve", lambda e,
                                 o=a_vr[:, vslot * c.D:(vslot + 1) * c.D],
                                 i=ap[j][:, :]: e.tensor_copy(o, i))
                            cap_evj[j] = (b.now("ve"), b.now("se"))
                            continue
                        b.wait("ve", "te", cap_proj)
                        b.op("ve", lambda e, o=ln["bn6"][:, 0:6], i=ap[j][:, 0:512]:
                             e.bn_stats(o, i))
                        b.op("ve", lambda e, o=ln["bn6"][:, 6:12], i=ap[j][:, 512:1024]:
                             e.bn_stats(o, i))
                        b.op("ve", lambda e, o=ln["mv"], i=ln["bn6"]: e.bn_aggr(o, i))
                        b.op("ve", lambda e, o=ln["m2"], a=ln["mv"][:, 0:1],
                             x=ln["mv"][:, 0:1]: e.tensor_tensor(o, a, x, ALU.mult))
                        b.op("ve", lambda e, o=ln["ex2"], a=ln["mv"][:, 1:2],
                             x=ln["m2"]: e.tensor_tensor(o, a, x, ALU.add))
                        b.wait("se", "ve")
                        b.op("se", lambda e, o=ln["std"], i=ln["ex2"]:
                             e.activation(o, i, ACTF.Sqrt, bias=c_eps_rms,
                                          scale=1.0))
                        b.wait("ve", "se")
                        b.op("ve", lambda e, o=ln["rstd"], i=ln["std"]:
                             e.reciprocal(o, i))
                        b.op("ve", lambda e, o=a_xln[:], i=ap[j][:, :], s=ln["rstd"]:
                             e.tensor_scalar(o, i, s, None, ALU.mult))
                        vcap = b.now("ve")
                        nw = qnw if proj == "q" else knw
                        for g4 in range(2):
                            b.wait("te", "ve", vcap if g4 == 0 else b.now("ve"))
                            for ic4 in range(4):
                                ic = g4 * 4 + ic4
                                b.op("te", lambda e,
                                     o=ap[j][:, ic4 * 128:(ic4 + 1) * 128],
                                     s=a_xln[:, ic * 128:(ic + 1) * 128],
                                     st=(ic4 == 0), sp=(ic4 == 3):
                                     e.matmul(o, s, id32[:, :], start=st, stop=sp,
                                              is_transpose=True), inc=(ic4 == 3))
                            tcap = b.now("te")
                            b.wait("ve", "te", tcap)
                            for ic4 in range(4):
                                ic = g4 * 4 + ic4
                                if proj == "q":
                                    dst = a_qnT[:].rearrange(
                                        "p (ic t) -> p ic t", ic=8)[
                                        :, ic, j * 128:(j + 1) * 128]
                                else:
                                    col = (q0q + j * 128) % c.RING
                                    dst = a_knr[:].rearrange(
                                        "p (ic t) -> p ic t", ic=8)[
                                        :, ic, col:col + 128]
                                b.op("ve", lambda e, o=dst,
                                     i=ap[j][:, ic4 * 128:(ic4 + 1) * 128],
                                     s=nw[:, ic:ic + 1]:
                                     e.tensor_scalar(o, i, s, None, ALU.mult))
                            cap_evj[j] = (b.now("ve"), b.now("se"))

                b.wait("te", "ve")

                def scores_ops(ph, j):
                    q0 = q0q + j * 128
                    kbeg_ = max(0, q0 - c.WINDOW)
                    kend = q0 + 128
                    segs = seglist(kbeg_, kend)
                    bank_first = {}
                    bank_last = {}
                    for s_ in segs:
                        bk = s_[0] // 512
                        bank_first.setdefault(bk, s_)
                        bank_last[bk] = s_
                    ops = []
                    for s_ in segs:
                        (off, col, take) = s_
                        for i01 in (0, 1):
                            hp = 64 * i01
                            ht = ph
                            lhs = a_qnT[hp:hp + 64,
                                        ht * c.QUARTER + j * 128:
                                        ht * c.QUARTER + (j + 1) * 128]
                            rhs = a_knr[hp:hp + 64,
                                        ht * c.RING + col:ht * c.RING + col + take]

                            def op_(o=scps[i01][:, off:off + take], l=lhs, r=rhs,
                                    st=(bank_first[off // 512] == s_),
                                    sp=(bank_last[off // 512] == s_),
                                    last=(s_ == segs[-1] and i01 == 1)):
                                b.op("te", lambda e, o=o, l=l, r=r, st=st, sp=sp:
                                     e.matmul(o, l, r, start=st, stop=sp), inc=last)
                            ops.append(op_)
                    return ops

                def pt_head(pend, i01):
                    (pj, pph, pnk, pnkc, pkbeg, ppar2) = pend
                    h = 2 * pph + i01
                    buf = a_pn[i01][ppar2]
                    b.wait("te", "se", cap_ptsb.get((h - 1) % 16, 0))
                    b.wait("te", "ve", cap_pn[h])
                    for kc in range(pnkc):
                        st = (kc % 4 == 0)
                        sp = (kc % 4 == 3) or (kc == pnkc - 1)
                        b.op("te", lambda e,
                             o=ap1[:, kc * 128:(kc + 1) * 128],
                             s=buf[:, kc * 128:(kc + 1) * 128],
                             st=st, sp=sp:
                             e.matmul(o, s, id32[:, :], start=st, stop=sp,
                                      is_transpose=True),
                             inc=(kc == pnkc - 1))
                    cap_ptT[h] = b.now("te")
                    b.wait("se", "te", cap_ptT[h])
                    b.op("se", lambda e, o=a_pt[i01][:, 0:pnk],
                         i=ap1[:, 0:pnk]: e.copy(o, i))
                    cap_ptsb[h] = b.now("se")

                def av_pair(pend):
                    (pj, pph, pnk, pnkc, pkbeg, ppar2) = pend
                    h1_ = 2 * pph + 1
                    b.wait("te", "se", cap_aTe.get((pph - 1) % 8, 0))
                    for i01 in (0, 1):
                        h = 2 * pph + i01
                        avc = i01 * 512
                        b.wait("te", "se", cap_ptsb[h])
                        for kc in range(pnkc):
                            vslot = ((pkbeg + kc * 128) // 128) % 8
                            b.op("te", lambda e,
                                 o=ap0[i01 * 64:(i01 + 1) * 64, avc:avc + 128],
                                 l=a_vr[:, vslot * c.D + h * 64:
                                        vslot * c.D + (h + 1) * 64],
                                 r=a_pt[i01][:, kc * 128:(kc + 1) * 128],
                                 st=(kc == 0), sp=(kc == pnkc - 1),
                                 tp=(0, 64 * i01):
                                 e.matmul(o, l, r, start=st, stop=sp,
                                          tile_position=tp),
                                 inc=(kc == pnkc - 1 and i01 == 1))
                    cap_avp[pph] = b.now("te")
                    b.wait("se", "te", cap_avp[pph])
                    for i01 in (0, 1):
                        hp = 64 * i01
                        avc = i01 * 512
                        b.op("se", lambda e,
                             o=a_aT4[hp:hp + 64,
                                     pj * 1024 + pph * 128:pj * 1024 + (pph + 1) * 128],
                             i=ap0[hp:hp + 64, avc:avc + 128]: e.copy(o, i))
                    cap_aTe[pph] = b.now("se")

                pend = None
                for j in range(cq):
                    q0 = q0q + j * 128
                    kbeg = max(0, q0 - c.WINDOW)
                    nk = q0 + 128 - kbeg
                    nkc = nk // 128
                    coff = 640 - nk
                    if j == 0:
                        b.wait("te", "ve")
                        b.wait("te", "se")
                        for op_ in scores_ops(0, 0):
                            op_()
                        cap_sc[0] = cap_sc[1] = b.now("te")
                    for ph in range(c.H // 2):
                        gpq = j * 8 + ph
                        par2 = gpq % 2
                        h0, h1 = 2 * ph, 2 * ph + 1
                        for i01 in (0, 1):
                            h = 2 * ph + i01
                            buf = a_pn[i01][par2]
                            b.wait("ve", "te", cap_sc[h])
                            b.wait("ve", "se", cap_exp.get((h - 4) % 16, 0))
                            b.wait("ve", "te", cap_ptT.get((h - 4) % 16, 0))
                            b.op("ve", lambda e, o=buf[:, 0:nk],
                                 i=dist_m[:, coff:640], sl=8.0 * c.slopes[h],
                                 x=scps[i01][:, 0:nk]:
                                 e.scalar_tensor_tensor(o, i, sl, x,
                                                        ALU.mult, ALU.add))
                            cap_stt[h] = b.now("ve")
                            b.op("ve", lambda e, o=a_st[:, i01 * 4:i01 * 4 + 1],
                                 i=buf[:, 0:nk]:
                                 e.reduce_max(o, i, AX, negate=True))
                            b.op("ve", lambda e, o=a_st[:, i01 * 4 + 3:i01 * 4 + 4],
                                 i=a_st[:, i01 * 4:i01 * 4 + 1]:
                                 e.tensor_scalar(o, i, 0.125, None, ALU.mult))
                            b.wait("se", "ve")
                            b.op("se", lambda e, o=buf[:, 0:nk], i=buf[:, 0:nk],
                                 bi=a_st[:, i01 * 4 + 3:i01 * 4 + 4],
                                 a=a_st[:, i01 * 4 + 1:i01 * 4 + 2]:
                                 e.activation(o, i, ACTF.Exp, bias=bi, scale=0.125,
                                              accum_out=a))
                            cap_exp[h] = b.now("se")
                            b.wait("ve", "se", cap_exp[h])
                            b.op("ve", lambda e, o=a_st[:, i01 * 4 + 2:i01 * 4 + 3],
                                 i=a_st[:, i01 * 4 + 1:i01 * 4 + 2]:
                                 e.reciprocal(o, i))
                            b.op("ve", lambda e, o=buf[:, 0:nk], i=buf[:, 0:nk],
                                 s=a_st[:, i01 * 4 + 2:i01 * 4 + 3]:
                                 e.tensor_scalar(o, i, s, None, ALU.mult))
                            cap_pn[h] = b.now("ve")
                        if ph + 1 < c.H // 2:
                            nxt = (ph + 1, j)
                        elif j + 1 < cq:
                            nxt = (0, j + 1)
                        else:
                            nxt = None
                        ops = scores_ops(*nxt) if nxt else []
                        if ops:
                            b.wait("te", "ve", cap_stt[h1])
                            ops[0]()
                        if pend is not None:
                            pt_head(pend, 0)
                        for op_ in ops[1:]:
                            op_()
                        if pend is not None:
                            pt_head(pend, 1)
                        if nxt is not None:
                            nh = 2 * nxt[0]
                            cap_sc[nh % 16] = cap_sc[(nh + 1) % 16] = b.now("te")
                        if pend is not None:
                            av_pair(pend)
                        pend = (j, ph, nk, nkc, kbeg, par2)
                pt_head(pend, 0)
                pt_head(pend, 1)
                av_pair(pend)
                pend = None

                cap_wo = {}
                b.wait("sy", "te")
                b.dma("sy", a_w[0][:], woT_d[0:128, :], ch="awl0")
                cap_wo[0] = b.now("awl0")
                b.dma("sy", a_w[1][:], woT_d[128:256, :], ch="awl1")
                cap_wo[1] = b.now("awl1")
                b.wait("te", "se")
                for jc in range(8):
                    b.wait("te", f"awl{jc % 2}", cap_wo[jc])
                    for j in range(cq):
                        for hh in range(2):
                            b.op("te", lambda e,
                                 o=ap[j][:, hh * 512:(hh + 1) * 512],
                                 l=a_aT4[:, j * 1024 + jc * 128:
                                         j * 1024 + (jc + 1) * 128],
                                 r=a_w[jc % 2][:, hh * 512:(hh + 1) * 512],
                                 st=(jc == 0), sp=(jc == 7):
                                 e.matmul(o, l, r, start=st, stop=sp),
                                 inc=(j == cq - 1 and hh == 1))
                    if jc + 2 < 8:
                        b.wait("sy", "te")
                        b.dma("sy", a_w[jc % 2][:],
                              woT_d[(jc + 2) * 128:(jc + 3) * 128, :],
                              ch=f"awl{jc % 2}")
                        cap_wo[jc + 2] = b.now(f"awl{jc % 2}")
                cap_womm = b.now("te")
                for j in range(cq):
                    tc = qq * cq + j
                    b.wait("ve", "te", cap_womm)
                    b.op("ve", lambda e, o=x_res[:, tc * c.D:(tc + 1) * c.D],
                         i=ap[j][:, :], x=x_res[:, tc * c.D:(tc + 1) * c.D]:
                         e.tensor_tensor(o, i, x, ALU.add))
            b.barrier()

    ffn_call()
    attn_call()
    ffn_call()

    b.barrier()
    for tc in range(n_tc):
        b.dma("ge", yout[tc * 128:(tc + 1) * 128, :],
              x_res[:, tc * c.D:(tc + 1) * c.D], ch="yout")
    sem, val = b.ch["yout"][0], b.now("yout")
    b.rec["ge"].append(lambda e, sem=sem, v=val: e.wait_ge(sem, v))

    with nc.Block() as block:
        @block.sync
        def _(eng):
            for fn in b.rec["sy"]:
                fn(eng)

        @block.tensor
        def _(eng):
            for fn in b.rec["te"]:
                fn(eng)

        @block.vector
        def _(eng):
            for fn in b.rec["ve"]:
                fn(eng)

        @block.scalar
        def _(eng):
            for fn in b.rec["se"]:
                fn(eng)

        @block.gpsimd
        def _(eng):
            for fn in b.rec["ge"]:
                fn(eng)

    stack.close()
    return nc


def make_consts(cfg: CFG):
    id32 = np.eye(128, dtype=np.float32)
    idbf = np.eye(128, dtype=ml_dtypes.bfloat16)
    r = np.arange(128)[:, None]
    ccol = np.arange(640)[None, :]
    dist = (ccol - 512 - r).astype(np.float32)
    valid = (dist <= 0) & (dist >= -cfg.WINDOW)
    dist_m = np.where(valid, dist, np.float32(-3e37)).astype(np.float32)
    return id32, idbf, dist_m


def make_inmaps(cfg, x, wq, wk, wv, wo, qnw, knw, wg, wu, wd):
    id32, idbf, dist_m = make_consts(cfg)
    f32 = lambda a: np.ascontiguousarray(np.asarray(a, dtype=np.float32))
    common = dict(wq=f32(wq), wk=f32(wk), wv=f32(wv), wo=f32(wo),
                  qnw=f32(qnw), knw=f32(knw), wg=f32(wg), wu=f32(wu), wd=f32(wd),
                  id32=id32, idbf=idbf, dist_m=dist_m)
    x = np.asarray(x, dtype=np.float32)
    return [dict(common, x=np.ascontiguousarray(x[i])) for i in range(x.shape[0])]


def kernel(L0_attn_x, L0_attn_wq, L0_attn_wk, L0_attn_wv, L0_attn_wo,
           L0_attn_q_norm_w, L0_attn_k_norm_w, L0_ffn_w_gate, L0_ffn_w_up,
           L0_ffn_w_down):
    cfg = CFG()
    nc = build_core(cfg)
    in_maps = make_inmaps(cfg, L0_attn_x, L0_attn_wq, L0_attn_wk, L0_attn_wv,
                          L0_attn_wo, L0_attn_q_norm_w, L0_attn_k_norm_w,
                          L0_ffn_w_gate, L0_ffn_w_up, L0_ffn_w_down)
    res = run_bass_kernel_spmd(nc, in_maps, core_ids=list(range(cfg.N_CORES)))
    return np.stack([r["y"] for r in res.results], axis=0)


# version 22 (final)
# speedup vs baseline: 1.1354x; 1.1354x over previous
import numpy as np
import ml_dtypes

import concourse.bass as bass
import concourse.mybir as mybir
from concourse.bass_utils import run_bass_kernel_spmd
from contextlib import ExitStack

F32 = mybir.dt.float32
BF16 = mybir.dt.bfloat16
I32 = mybir.dt.int32
AX = mybir.AxisListType.X
ALU = mybir.AluOpType
ACTF = mybir.ActivationFunctionType

MAGIC = 256.0

ENGS = ("sy", "te", "ve", "se", "ge")


class CFG:
    T = 2048
    D = 1024
    INNER = 4096
    H = 16
    HD = 64
    WINDOW = 512
    TT = 256
    QUARTER = 512
    RING = 1024
    N_CORES = 8
    slopes = [2.0 ** (-(i + 1) / 2.0) for i in range(16)]


def _ap_ranges(fn):
    out = []
    for v in (fn.__defaults__ or ()):
        try:
            t = v.tensor
            name = t.name
            ap = v.ap
            off = int(v.offset)
            span = 1
            dims = list(ap)
            for d in dims[1:]:
                span += int(d[0]) * (int(d[1]) - 1)
            out.append((name, off, off + span))
        except Exception:
            try:
                out.append((v.tensor.name, 0, 1 << 60))
            except Exception:
                pass
    return out


def _overlap(a, b):
    return a[0] == b[0] and a[1] < b[2] and b[1] < a[2]


class B:
    def __init__(self, nc, stack):
        self.nc = nc
        self.stack = stack
        self.rec = {e: [] for e in ENGS}
        self.ch = {}
        for e in ENGS:
            self.chan(e)
        self.lastw = {}
        self.pend = {e: [] for e in ENGS}

    def chan(self, name):
        if name not in self.ch:
            self.ch[name] = [self.stack.enter_context(
                self.nc.semaphore(f"c_{name}")), 0]
        return name

    def op(self, eng, fn, inc=True, amt=1, nodep=False):
        if eng in ("ve", "se", "ge") and not nodep:
            rng = _ap_ranges(fn)
            pend = self.pend[eng]
            if not rng or any(_overlap(a, pb) for a in rng for pb in pend):
                self.wait(eng, eng)
                pend.clear()
            pend.extend(rng)
        if inc:
            name = eng if inc is True else self.chan(inc)
            sem = self.ch[name][0]
            self.ch[name][1] += amt
            self.rec[eng].append(lambda e, fn=fn, sem=sem, amt=amt:
                                 fn(e).then_inc(sem, amt))
        else:
            self.rec[eng].append(fn)

    def dma(self, eng, out_ap, in_ap, slow=False, ch=None):
        self.op(eng, lambda e, o=out_ap, i=in_ap, s=slow: e.dma_start(
            out=o, in_=i, allow_slow_non_contiguous=s), amt=16,
            inc=(True if ch is None else ch))

    def now(self, name):
        return self.ch[name][1]

    def wait(self, eng, prod, val=None):
        self.chan(prod)
        if val is None:
            val = self.ch[prod][1]
        if val <= 0 or self.lastw.get((eng, prod), 0) >= val:
            return
        self.lastw[(eng, prod)] = val
        sem = self.ch[prod][0]
        self.rec[eng].append(lambda e, sem=sem, v=val: e.wait_ge(sem, v))

    def selfwait(self, eng):
        self.wait(eng, eng)
        if eng in self.pend:
            self.pend[eng].clear()

    def barrier(self):
        for e in ENGS:
            for p in self.ch:
                if p != e:
                    self.wait(e, p)


def build_core(cfg: CFG):
    nc = bass.Bass("TRN2", target_bir_lowering=False)
    stack = ExitStack()
    c = cfg
    n_tc = c.T // 128

    xin = nc.dram_tensor("x", [c.T, c.D], F32, kind="ExternalInput").ap()
    w_q = nc.dram_tensor("wq", [c.D, c.D], F32, kind="ExternalInput").ap()
    w_k = nc.dram_tensor("wk", [c.D, c.D], F32, kind="ExternalInput").ap()
    w_v = nc.dram_tensor("wv", [c.D, c.D], F32, kind="ExternalInput").ap()
    w_o = nc.dram_tensor("wo", [c.D, c.D], F32, kind="ExternalInput").ap()
    qnw_in = nc.dram_tensor("qnw", [c.D], F32, kind="ExternalInput").ap()
    knw_in = nc.dram_tensor("knw", [c.D], F32, kind="ExternalInput").ap()
    w_g = nc.dram_tensor("wg", [c.INNER, c.D], F32, kind="ExternalInput").ap()
    w_u = nc.dram_tensor("wu", [c.INNER, c.D], F32, kind="ExternalInput").ap()
    w_d = nc.dram_tensor("wd", [c.D, c.INNER], F32, kind="ExternalInput").ap()
    id32_in = nc.dram_tensor("id32", [128, 128], F32, kind="ExternalInput").ap()
    idbf_in = nc.dram_tensor("idbf", [128, 128], BF16, kind="ExternalInput").ap()
    dist_in = nc.dram_tensor("dist_m", [128, 640], F32, kind="ExternalInput").ap()
    yout = nc.dram_tensor("y", [c.T, c.D], F32, kind="ExternalOutput").ap()

    wgT_d = nc.dram_tensor("wgT", [c.D, c.INNER], BF16).ap()
    wuT_d = nc.dram_tensor("wuT", [c.D, c.INNER], BF16).ap()
    wdT_d = nc.dram_tensor("wdT", [c.INNER, c.D], BF16).ap()
    wqT_d = nc.dram_tensor("wqTs", [c.D, c.D], F32).ap()
    wkT_d = nc.dram_tensor("wkTs", [c.D, c.D], F32).ap()
    wvT_d = nc.dram_tensor("wvTs", [c.D, c.D], F32).ap()
    woT_d = nc.dram_tensor("woTs", [c.D, c.D], F32).ap()

    x_res = stack.enter_context(nc.sbuf_tensor("x_res", [128, n_tc * c.D], F32))
    id32 = stack.enter_context(nc.sbuf_tensor("id32s", [128, 128], F32))
    idbf = stack.enter_context(nc.sbuf_tensor("idbfs", [128, 128], BF16))
    qnw = stack.enter_context(nc.sbuf_tensor("qnw_s", [128, 8], F32))
    knw = stack.enter_context(nc.sbuf_tensor("knw_s", [128, 8], F32))
    wsc = stack.enter_context(nc.sbuf_tensor("wsc", [128, 8], F32))
    ones4 = stack.enter_context(nc.sbuf_tensor("ones4", [4, 128], F32))
    cpool = stack.enter_context(nc.sbuf_tensor("cpool", [128, 4], F32))
    ln = {}
    for k in ("rsum", "ssq", "m", "ex2", "m2", "var", "std", "rstd", "mrstd"):
        ln[k] = stack.enter_context(nc.sbuf_tensor(f"ln_{k}", [128, 1], F32))[:]
    ln["bn6"] = stack.enter_context(nc.sbuf_tensor("ln_bn6", [128, 12], F32))[:]
    ln["mv"] = stack.enter_context(nc.sbuf_tensor("ln_mv", [128, 2], F32))[:]

    b = B(nc, stack)

    b.dma("sy", id32[:], id32_in[:, :], ch="ini")
    b.dma("sy", idbf[:], idbf_in[:, :], ch="ini")
    b.dma("sy", qnw[:], qnw_in.rearrange("(c p) -> p c", p=128), slow=True, ch="ini")
    b.dma("sy", knw[:], knw_in.rearrange("(c p) -> p c", p=128), slow=True, ch="ini")
    for tc in range(n_tc):
        b.dma("ge", x_res[:, tc * c.D:(tc + 1) * c.D], xin[tc * 128:(tc + 1) * 128, :],
              ch="xin")
    b.wait("te", "ini")
    b.wait("ve", "ini")
    b.op("ve", lambda e: e.memset(ones4[:], 1.0))
    b.op("ve", lambda e: e.memset(cpool[:, 0:1], 1e-5))
    b.op("ve", lambda e: e.memset(cpool[:, 1:2], 1e-6))
    c_eps_ln = cpool[:, 0:1]
    c_eps_rms = cpool[:, 1:2]

    with (
        nc.sbuf_tensor("pw_t0", [128, 4096], F32) as pw_t0,
        nc.sbuf_tensor("pw_t1", [128, 4096], F32) as pw_t1,
        nc.sbuf_tensor("pw_sums", [128, 28], F32) as pw_sums,
        nc.sbuf_tensor("pw_tot", [4, 4], F32) as pw_tot,
        nc.sbuf_tensor("pw_diag", [4, 8], F32) as pw_diag,
        nc.psum_tensor("pw_ps", [128, 512], F32) as pw_ps,
    ):
        pw_t = [pw_t0, pw_t1]
        b.op("ve", lambda e: e.memset(pw_sums[:, 24:28], 0.0))
        slabs = []
        for widx, wap in enumerate((w_g, w_u)):
            wr = wap.rearrange("(s q p) n -> s p q n", p=128, q=4)
            for s in range(8):
                slabs.append((widx, s, wr[s]))
        for s in range(8):
            slabs.append((2, s, w_d[s * 128:(s + 1) * 128, :]
                          .rearrange("p (q n) -> p q n", q=4)))
        cap_red = {}
        cap_dma = {}
        for i, (widx, s, src) in enumerate(slabs):
            sl = i % 2
            if i >= 2:
                b.wait("sy", "ve", cap_red.get(i - 2, 0))
            b.dma("sy", pw_t[sl][:].rearrange("p (q n) -> p q n", q=4), src,
                  ch=f"pwl{sl}")
            cap_dma[i] = b.now(f"pwl{sl}")
            b.wait("ve", f"pwl{sl}", cap_dma[i])
            b.op("ve", lambda e, o=pw_sums[:, widx * 8 + s:widx * 8 + s + 1],
                 i=pw_t[sl][:]:
                 e.reduce_sum(o, i, AX, apply_absolute_value=True))
            cap_red[i] = b.now("ve")
        for widx in range(3):
            b.op("ve", lambda e, o=pw_sums[:, 24 + widx:25 + widx],
                 i=pw_sums[:, widx * 8:widx * 8 + 8]:
                 e.reduce_sum(o, i, AX))
        b.wait("te", "ve")
        b.op("te", lambda e: e.matmul(pw_ps[0:4, 0:128], pw_sums[:, 24:28],
                                      id32[:, :], start=True, stop=True,
                                      is_transpose=True))
        b.wait("ve", "te")
        b.op("ve", lambda e: e.reduce_sum(pw_tot[:, 0:1], pw_ps[0:4, 0:128], AX))
        b.op("ve", lambda e: e.tensor_scalar(pw_tot[:, 1:2], pw_tot[:, 0:1],
                                             1.0 / (c.INNER * c.D), 1e-5,
                                             ALU.mult, ALU.max))
        b.op("ve", lambda e: e.reciprocal(pw_tot[:, 2:3], pw_tot[:, 1:2]))
        b.op("ve", lambda e: e.tensor_tensor(pw_diag[:, 0:4], id32[0:4, 0:4],
                                             pw_tot[:, 1:2].broadcast_to((4, 4)),
                                             ALU.mult))
        b.op("ve", lambda e: e.tensor_tensor(pw_diag[:, 4:8], id32[0:4, 0:4],
                                             pw_tot[:, 2:3].broadcast_to((4, 4)),
                                             ALU.mult))
        b.wait("te", "ve")
        b.op("te", lambda e: e.matmul(pw_ps[:, 0:8], ones4[:, :], pw_diag[:, 0:8],
                                      start=True, stop=True))
        b.wait("ve", "te")
        b.op("ve", lambda e: e.tensor_copy(wsc[:, 0:4], pw_ps[:, 0:4]))
        b.op("ve", lambda e: e.tensor_copy(wsc[:, 4:8], pw_ps[:, 4:8]))
        b.barrier()

    def tern_transpose(wap, wT_dst, widx, rows, cols):
        nrb = rows // 128
        nch = cols // 512
        nm = f"tt{widx}"
        sw_ap = wsc[:, 4 + widx:5 + widx]
        nring = 4 if nrb > 8 else 2
        spb = nrb // 4
        with (
            nc.sbuf_tensor(nm + "_w", [128, nring * 2048], F32) as tt_w,
            nc.sbuf_tensor(nm + "_t", [128, nrb * 512], BF16) as tt_t,
            nc.sbuf_tensor(nm + "_abs", [128, 4096], F32) as tt_abs,
            nc.sbuf_tensor(nm + "_sgn", [128, 4096], F32) as tt_sgn,
            nc.sbuf_tensor(nm + "_asm", [128, 2 * rows], BF16) as tt_asm,
            nc.psum_tensor(nm + "_ps", [128, 2048], BF16) as tt_ps,
        ):
            cap_half_T = 0
            cap_wb = {}
            cap_cp = {}
            cap_seq = {}
            cap_vq = {}
            gg_ctr = [0]
            for h in range(nch):
                first_t = [True]
                for s in range(spb):
                    gs = h * spb + s
                    b.wait("sy", "se", cap_seq.get(gs - 2, 0))
                    slot = (gs % nring) * 2048
                    src = wap.rearrange("(r q p) n -> r p q n", p=128, q=4)[
                        s][:, :, h * 512:(h + 1) * 512]
                    chn = f"ttw{gs % 2}"
                    b.dma("sy", tt_w[:, slot:slot + 2048]
                          .rearrange("p (q n) -> p q n", q=4), src, ch=chn)
                    dcap = b.now(chn)
                    par = (gs % 2) * 2048
                    b.wait("se", chn, dcap)
                    b.wait("se", "ve", cap_vq.get(gs - 2, 0))
                    b.op("se", lambda e, o=tt_abs[:, par:par + 2048],
                         i=tt_w[:, slot:slot + 2048]:
                         e.activation(o, i, ACTF.Abs))
                    b.op("se", lambda e, o=tt_sgn[:, par:par + 2048],
                         i=tt_w[:, slot:slot + 2048]:
                         e.activation(o, i, ACTF.Sign))
                    cap_seq[gs] = b.now("se")
                    b.wait("ve", "se", cap_seq[gs])
                    b.op("ve", lambda e, o=tt_abs[:, par:par + 2048],
                         i=tt_abs[:, par:par + 2048], s_=sw_ap:
                         e.tensor_scalar(o, i, s_, 0.5, ALU.mult, ALU.is_ge))
                    if first_t[0]:
                        b.wait("ve", "te", cap_half_T)
                        first_t[0] = False
                    b.op("ve", lambda e, o=tt_t[:, s * 2048:(s + 1) * 2048],
                         a=tt_abs[:, par:par + 2048], s_=tt_sgn[:, par:par + 2048]:
                         e.tensor_tensor(o, a, s_, ALU.mult))
                    cap_vq[gs] = b.now("ve")
                cap_half_quant = cap_vq[h * spb + spb - 1]
                for cg in range(4):
                    gc = h * 4 + cg
                    s2 = gc % 2
                    for rb8 in range(0, nrb, 8):
                        gg = gg_ctr[0]
                        gg_ctr[0] += 1
                        pp = (gg % 2) * 1024
                        b.wait("te", "ve",
                               max(cap_half_quant, cap_cp.get(gg - 2, 0)))
                        for j in range(8):
                            rb = rb8 + j
                            src = tt_t[:, rb * 512 + cg * 128:rb * 512 + (cg + 1) * 128]
                            b.op("te", lambda e, o=tt_ps[:, pp + j * 128:pp + (j + 1) * 128],
                                 s=src, st=(j == 0), sp=(j == 7):
                                 e.matmul(o, s, idbf[:, :], start=st, stop=sp,
                                          is_transpose=True), inc=(j == 7))
                        b.wait("ve", "te")
                        if rb8 == 0:
                            b.wait("ve", f"wb{s2}", cap_wb.get(gc - 2, 0))
                        b.op("ve", lambda e,
                             o=tt_asm[:, s2 * rows + rb8 * 128:s2 * rows + (rb8 + 8) * 128],
                             i=tt_ps[:, pp:pp + 1024]: e.tensor_copy(o, i))
                        cap_cp[gg] = b.now("ve")
                    b.selfwait("ve")
                    b.wait("ge", "ve")
                    row = h * 512 + cg * 128
                    b.dma("ge", wT_dst[row:row + 128, :],
                          tt_asm[:, s2 * rows:(s2 + 1) * rows], ch=f"wb{s2}")
                    cap_wb[gc] = b.now(f"wb{s2}")
                cap_half_T = b.now("te")
                cap_half_vq = cap_vq[nrb // 4 - 1]
            b.barrier()

    tern_transpose(w_g, wgT_d, 0, c.INNER, c.D)
    tern_transpose(w_u, wuT_d, 1, c.INNER, c.D)

    with (
        nc.sbuf_tensor("at_w0", [128, 8 * 1024], F32) as at_w0,
        nc.sbuf_tensor("at_w1", [128, 8 * 1024], F32) as at_w1,
        nc.sbuf_tensor("at_asm", [128, 2048], F32) as at_asm,
        nc.psum_tensor("at_ps", [128, 1024], F32) as at_ps,
    ):
        at_w = [at_w0, at_w1]
        cap_at_te = [0, 0]
        cap_aw = {}
        cap_acp = {}
        agg = [0]
        for wi, (wap, wT_dst) in enumerate(((w_q, wqT_d), (w_k, wkT_d),
                                            (w_v, wvT_d), (w_o, woT_d))):
            sl = wi % 2
            b.wait("sy", "te", cap_at_te[sl])
            for rb in range(8):
                b.dma("sy", at_w[sl][:, rb * 1024:(rb + 1) * 1024],
                      wap[rb * 128:(rb + 1) * 128, :], ch=f"awl{sl}")
            dcap = b.now(f"awl{sl}")
            b.wait("te", f"awl{sl}", dcap)
            for r in range(8):
                ar = wi * 8 + r
                s2 = ar % 2
                for g4 in range(2):
                    gg = agg[0]
                    agg[0] += 1
                    pp = (gg % 2) * 512
                    b.wait("te", "ve", cap_acp.get(gg - 2, 0))
                    for j in range(4):
                        rb = g4 * 4 + j
                        src = at_w[sl][:, rb * 1024 + r * 128:rb * 1024 + (r + 1) * 128]
                        b.op("te", lambda e, o=at_ps[:, pp + j * 128:pp + (j + 1) * 128],
                             s=src, st=(j == 0), sp=(j == 3):
                             e.matmul(o, s, id32[:, :], start=st, stop=sp,
                                      is_transpose=True), inc=(j == 3))
                    b.wait("ve", "te")
                    if g4 == 0:
                        b.wait("ve", f"aw{s2}", cap_aw.get(ar - 2, 0))
                    b.op("ve", lambda e,
                         o=at_asm[:, s2 * 1024 + g4 * 512:s2 * 1024 + (g4 + 1) * 512],
                         i=at_ps[:, pp:pp + 512]: e.tensor_copy(o, i))
                    cap_acp[gg] = b.now("ve")
                b.selfwait("ve")
                b.wait("ge", "ve")
                b.dma("ge", wT_dst[r * 128:(r + 1) * 128, :],
                      at_asm[:, s2 * 1024:(s2 + 1) * 1024], ch=f"aw{s2}")
                cap_aw[ar] = b.now(f"aw{s2}")
            cap_at_te[sl] = b.now("te")
        tern_transpose(w_d, wdT_d, 2, c.D, c.INNER)
        b.barrier()

    def layer_norm(tc, dst_ap):
        src = x_res[:, tc * c.D:(tc + 1) * c.D]
        b.op("ve", lambda e, o=ln["bn6"][:, 0:6], i=src[:, 0:512]: e.bn_stats(o, i))
        b.op("ve", lambda e, o=ln["bn6"][:, 6:12], i=src[:, 512:1024]: e.bn_stats(o, i))
        b.op("ve", lambda e, o=ln["mv"], i=ln["bn6"]: e.bn_aggr(o, i))
        b.wait("se", "ve")
        b.op("se", lambda e, o=ln["std"], i=ln["mv"][:, 1:2]:
             e.activation(o, i, ACTF.Sqrt, bias=c_eps_ln, scale=1.0))
        b.wait("ve", "se")
        b.op("ve", lambda e, o=ln["rstd"], i=ln["std"]: e.reciprocal(o, i))
        b.op("ve", lambda e, o=ln["mrstd"], a=ln["mv"][:, 0:1], x=ln["rstd"]:
             e.tensor_tensor(o, a, x, ALU.mult))
        b.op("ve", lambda e, o=dst_ap, i=src, s=ln["rstd"], mb=ln["mrstd"]:
             e.scalar_tensor_tensor(o, i, s, mb.broadcast_to((128, c.D)),
                                    ALU.mult, ALU.subtract))

    MAGICRND = 12582912.0

    def quant_chain(src_ap, s_ap, dst_ap, y2=None, e2=None, q2i=None):
        b.op("ve", lambda e, o=src_ap, i=src_ap, s=s_ap:
             e.tensor_scalar(o, i, s, MAGICRND, ALU.mult, ALU.add))
        b.op("ve", lambda e, o=dst_ap, i=src_ap:
             e.tensor_scalar(o, i, MAGICRND, None, ALU.subtract))

    _f_ctr = [0]

    def ffn_call():
        ntt = c.T // c.TT
        ctc = c.TT // 128
        nm = f"f{_f_ctr[0]}"; _f_ctr[0] += 1
        with ExitStack() as fstk:
            E = fstk.enter_context
            f_xln = E(nc.sbuf_tensor(nm + "_xln", [128, c.D], F32))
            f_xqT = E(nc.sbuf_tensor(nm + "_xqT", [128, 8 * c.TT], BF16))
            f_g = E(nc.sbuf_tensor(nm + "_g", [128, ctc * c.INNER], F32))
            f_gq = E(nc.sbuf_tensor(nm + "_gq", [128, 2048], BF16))
            f_gqb = [f_gq, f_gq]
            f_gqT = E(nc.sbuf_tensor(nm + "_gqT", [128, 16 * c.TT], BF16))
            f_wg0 = E(nc.sbuf_tensor(nm + "_wg0", [128, 8192], BF16))
            f_wg1 = E(nc.sbuf_tensor(nm + "_wg1", [128, 8192], BF16))
            f_wu0 = E(nc.sbuf_tensor(nm + "_wu0", [128, 8192], BF16))
            f_wu1 = E(nc.sbuf_tensor(nm + "_wu1", [128, 8192], BF16))
            f_wdr = E(nc.sbuf_tensor(nm + "_wdr", [128, 4096], BF16))
            f_sg = E(nc.sbuf_tensor(nm + "_sg", [128, 2048], F32))
            f_amax = E(nc.sbuf_tensor(nm + "_amax", [128, 1], F32))
            f_sc = E(nc.sbuf_tensor(nm + "_sc", [128, 2], F32))
            f_oscs = E(nc.sbuf_tensor(nm + "_oscs", [128, 3 * n_tc], F32))
            f_ps = [E(nc.psum_tensor(nm + f"_ps{i}", [128, 512], F32))
                    for i in range(8)]
            f_pst_gu = f_ps[0][:].bitcast(BF16)
            f_pst_gq = f_ps[2][:].bitcast(BF16)
            f_wd = [f_wdr[:, i * 1024:(i + 1) * 1024] for i in range(4)]
            f_wgb = [f_wg0, f_wg1]
            f_wub = [f_wu0, f_wu1]

            def load_gu(pr):
                osl2 = pr % 4
                sl = pr % 2
                chn = f"wgl{sl}"
                b.dma("sy", f_wgb[sl][:].rearrange("p (ic w) -> p ic w", ic=8),
                      wgT_d.rearrange("(ic p) n -> p ic n", p=128)
                      [:, :, osl2 * 1024:(osl2 + 1) * 1024], ch=chn)
                b.dma("sy", f_wub[sl][:].rearrange("p (ic w) -> p ic w", ic=8),
                      wuT_d.rearrange("(ic p) n -> p ic n", p=128)
                      [:, :, osl2 * 1024:(osl2 + 1) * 1024], ch=chn)
                return (chn, b.now(chn))

            f_xqT1 = E(nc.sbuf_tensor(nm + "_xqT1", [128, 8 * c.TT], BF16))
            f_sq = E(nc.sbuf_tensor(nm + "_sq2", [128, 2], F32))
            f_gmax = E(nc.sbuf_tensor(nm + "_gmax", [128, 16], F32))
            f_xqTb = [f_xqT, f_xqT1]
            cap_lnqT = [0, 0]

            def lnq_compute(tt, j):
                tc = tt * ctc + j
                layer_norm(tc, f_xln[:])
                b.op("ve", lambda e, o=f_amax[:], i=f_xln[:]:
                     e.reduce_max(o, i, AX, apply_absolute_value=True))
                b.op("ve", lambda e, o=f_sc[:, 1:2], i=f_amax[:]:
                     e.tensor_scalar(o, i, 1e-5, 1.0 / 127.0, ALU.max, ALU.mult))
                b.op("ve", lambda e, o=f_sc[:, 0:1], i=f_sc[:, 1:2]:
                     e.reciprocal(o, i))
                b.op("ve", lambda e, o=f_oscs[:, 3 * tc:3 * tc + 1],
                     a=f_sc[:, 1:2], x=wsc[:, 0:1]:
                     e.tensor_tensor(o, a, x, ALU.mult))
                b.op("ve", lambda e, o=f_oscs[:, 3 * tc + 1:3 * tc + 2],
                     a=f_sc[:, 1:2], x=wsc[:, 1:2]:
                     e.tensor_tensor(o, a, x, ALU.mult))
                b.wait("ve", "te")
                quant_chain(f_xln[:], f_sc[:, 0:1],
                            f_gq[:, j * 1024:(j + 1) * 1024])
                return b.now("ve")

            def lnq_transpose(tt, j, pst, te_free=(0, 0), ve_ready=0):
                dst_x = f_xqTb[tt % 2]
                gqx = f_gq[:, j * 1024:(j + 1) * 1024]
                b.wait("te", "ve", ve_ready)
                b.wait("te", "ve", te_free[0])
                b.wait("te", "se", te_free[1])
                for ic in range(8):
                    b.op("te", lambda e, o=pst[:, ic * 128:(ic + 1) * 128],
                         s=gqx[:, ic * 128:(ic + 1) * 128],
                         st=(ic == 0), sp=(ic == 7):
                         e.matmul(o, s, idbf[:, :], start=st, stop=sp,
                                  is_transpose=True), inc=(ic == 7))
                tcap = b.now("te")
                b.wait("ve", "te", tcap)
                b.op("ve", lambda e,
                     o=dst_x[:].rearrange("p (ic t) -> p ic t", ic=8)
                     [:, :, j * 128:(j + 1) * 128],
                     i=pst[:, 0:1024].rearrange("p (ic t) -> p ic t", ic=8):
                     e.tensor_copy(o, i))
                return b.now("ve")

            def lnq_chunk(tt, j, pst=None, te_free=(0, 0)):
                if pst is None:
                    pst = f_pst_gu
                vr = lnq_compute(tt, j)
                return lnq_transpose(tt, j, pst, te_free, vr)

            def lnq_tile(tt):
                for j in range(ctc):
                    lnq_chunk(tt, j)

            lnq_tile(0)
            for tt in range(ntt):
                xq_cur = f_xqTb[tt % 2]
                b.wait("te", "ve")
                b.wait("te", "se")
                cap_wdma = {}
                cap_evac = {}
                cap_mm = {}
                cap_wdma[0] = load_gu(0)
                cap_wdma[1] = load_gu(1)
                gp = 0
                lnq_caps = [0, 0]
                for pr in range(4):
                    osl_a, osl_b = 2 * pr, 2 * pr + 1
                    wsl = pr % 2
                    for j in range(ctc):
                        tc = tt * ctc + j
                        grp = (gp % 2) * 4
                        tiles = f_ps[grp:grp + 4]
                        if pr == 3 and j == 0 and tt + 1 < ntt:
                            lcap = lnq_transpose(tt + 1, 0, f_pst_gu,
                                                 cap_evac[gp - 2], lnq_caps[0])
                            cap_evac[gp - 2] = (max(cap_evac[gp - 2][0], lcap),
                                                cap_evac[gp - 2][1])
                        b.wait("te", cap_wdma[pr][0], cap_wdma[pr][1])
                        if gp >= 2:
                            b.wait("te", "ve", cap_evac[gp - 2][0])
                            b.wait("te", "se", cap_evac[gp - 2][1])
                        for ic in range(8):
                            lhs = xq_cur[:].rearrange("p (ic t) -> p ic t", ic=8)[
                                :, ic, j * 128:(j + 1) * 128]
                            for q in range(4):
                                buf = f_wgb[wsl] if q % 2 == 0 else f_wub[wsl]
                                half = q // 2
                                b.op("te", lambda e, o=tiles[q][:], l=lhs,
                                     r=buf[:, ic * 1024 + half * 512:
                                           ic * 1024 + (half + 1) * 512],
                                     st=(ic == 0), sp=(ic == 7):
                                     e.matmul(o, l, r, start=st, stop=sp),
                                     inc=(ic == 7 and q == 3))
                        cap_mm[gp] = b.now("te")
                        if j == ctc - 1 and pr + 2 < 4:
                            b.wait("sy", "te", cap_mm[gp])
                            cap_wdma[pr + 2] = load_gu(pr + 2)
                        for sub in range(2):
                            osl = osl_a if sub == 0 else osl_b
                            pg, pu = tiles[2 * sub], tiles[2 * sub + 1]
                            b.wait("se", "te", cap_mm[gp])
                            sgo = (gp % 2) * 1024 + sub * 512
                            b.op("se", lambda e, o=f_sg[:, sgo:sgo + 512], i=pg[:],
                                 s=f_oscs[:, 3 * tc:3 * tc + 1]:
                                 e.activation(o, i, ACTF.Silu, scale=s))
                            secap = b.now("se")
                            b.wait("ve", "se", secap)
                            b.op("ve", lambda e,
                                 o=f_g[:, j * c.INNER + osl * 512:
                                       j * c.INNER + (osl + 1) * 512],
                                 i=pu[:], s=f_oscs[:, 3 * tc + 1:3 * tc + 2],
                                 x=f_sg[:, sgo:sgo + 512]:
                                 e.scalar_tensor_tensor(o, i, s, x, ALU.mult,
                                                        ALU.mult))
                            b.op("ve", lambda e,
                                 o=f_gmax[:, osl * 2 + j:osl * 2 + j + 1],
                                 i=f_g[:, j * c.INNER + osl * 512:
                                        j * c.INNER + (osl + 1) * 512]:
                                 e.reduce_max(o, i, AX, apply_absolute_value=True))
                        cap_evac[gp] = (b.now("ve"), b.now("se"))
                        gp += 1
                    if pr == 1 and tt + 1 < ntt:
                        lnq_caps[0] = lnq_compute(tt + 1, 0)
                    if pr == 2 and tt + 1 < ntt:
                        lnq_caps[1] = lnq_compute(tt + 1, 1)

                for j in range(ctc):
                    tc = tt * ctc + j
                    b.op("ve", lambda e, o=f_amax[:],
                         i=f_gmax[:, j:j + 16:2] if False else
                         f_gmax[:].rearrange("p (o j) -> p j o", j=2)[:, j, :]:
                         e.reduce_max(o, i, AX))
                    b.op("ve", lambda e, o=f_sc[:, 1:2], i=f_amax[:]:
                         e.tensor_scalar(o, i, 1e-5, 1.0 / 127.0, ALU.max, ALU.mult))
                    b.op("ve", lambda e, o=f_sq[:, j:j + 1], i=f_sc[:, 1:2]:
                         e.reciprocal(o, i))
                    b.op("ve", lambda e, o=f_oscs[:, 3 * tc + 2:3 * tc + 3],
                         a=f_sc[:, 1:2], x=wsc[:, 2:3]:
                         e.tensor_tensor(o, a, x, ALU.mult))
                    b.op("ve", lambda e, o=f_oscs[:, 3 * tc + 2:3 * tc + 3],
                         i=f_oscs[:, 3 * tc + 2:3 * tc + 3]:
                         e.tensor_scalar(o, i, 0.5, None, ALU.mult))

                cap_d = {}
                for jc0 in range(4):
                    b.dma("sy", f_wd[jc0][:], wdT_d[jc0 * 128:(jc0 + 1) * 128, :],
                          ch=f"wdl{jc0}")
                    cap_d[jc0] = b.now(f"wdl{jc0}")
                cap_dmm = {}
                caps_gqT_ev = {}
                if tt + 1 < ntt:
                    lnq_transpose(tt + 1, 1, f_pst_gq,
                                  (b.now("ve"), b.now("se")), lnq_caps[1])
                b.wait("ve", "te")
                last_T = [0]
                for sl in range(4):
                    for j in range(ctc):
                        gj = f_g[:, j * c.INNER:(j + 1) * c.INNER]
                        b.wait("ve", "te", last_T[0])
                        slq = (sl % 2) * 1024
                        quant_chain(gj[:, sl * 1024:(sl + 1) * 1024],
                                    f_sq[:, j:j + 1],
                                    f_gqb[j][:, slq:slq + 1024])
                        b.wait("te", "ve")
                        b.wait("te", "se")
                        for k in range(8):
                            b.op("te", lambda e, o=f_pst_gq[:, k * 128:(k + 1) * 128],
                                 s=f_gqb[j][:, slq + k * 128:slq + (k + 1) * 128],
                                 st=(k == 0), sp=(k == 7):
                                 e.matmul(o, s, idbf[:, :], start=st, stop=sp,
                                          is_transpose=True), inc=(k == 7))
                        tcap = b.now("te")
                        last_T[0] = tcap
                        b.wait("se", "te", tcap)
                        b.op("se", lambda e,
                             o=f_gqT[:].rearrange("p (jc t) -> p jc t", jc=16)
                             [:, (sl % 2) * 8:(sl % 2 + 1) * 8, j * 128:(j + 1) * 128],
                             i=f_pst_gq[:, 0:1024].rearrange("p (k t) -> p k t", k=8):
                             e.copy(o, i))
                    caps_gqT_ev[sl] = b.now("se")
                    b.wait("te", "se", caps_gqT_ev[sl])
                    if sl == 0:
                        b.wait("te", "ve")
                    for k in range(8):
                        jc = sl * 8 + k
                        slot = jc % 4
                        b.wait("te", f"wdl{slot}", cap_d[jc])
                        for j in range(ctc):
                            lhs = f_gqT[:].rearrange("p (q t) -> p q t", q=16)[
                                :, jc % 16, j * 128:(j + 1) * 128]
                            for hh in range(2):
                                if j == 0:
                                    dstp = (f_ps[6] if hh == 0 else f_ps[7])[:]
                                else:
                                    dstp = (f_ps[0] if hh == 0 else f_ps[1])[:]
                                b.op("te", lambda e, o=dstp, l=lhs,
                                     r=f_wd[slot][:, hh * 512:(hh + 1) * 512],
                                     st=(jc == 0), sp=(jc == 31):
                                     e.matmul(o, l, r, start=st, stop=sp),
                                     inc=(j == ctc - 1 and hh == 1))
                        cap_dmm[jc] = b.now("te")
                        if jc + 4 < 32:
                            b.wait("sy", "te", cap_dmm[jc])
                            b.dma("sy", f_wd[slot][:],
                                  wdT_d[(jc + 4) * 128:(jc + 5) * 128, :],
                                  ch=f"wdl{slot}")
                            cap_d[jc + 4] = b.now(f"wdl{slot}")
                b.wait("ve", "te")
                for j2 in range(2):
                    tcx = tt * ctc + j2
                    pair = (f_ps[6], f_ps[7]) if j2 == 0 else (f_ps[0], f_ps[1])
                    for hh in range(2):
                        b.op("ve", lambda e,
                             o=x_res[:, tcx * c.D + hh * 512:
                                     tcx * c.D + (hh + 1) * 512],
                             i=pair[hh][:, 0:512],
                             s=f_oscs[:, 3 * tcx + 2:3 * tcx + 3],
                             x=x_res[:, tcx * c.D + hh * 512:
                                     tcx * c.D + (hh + 1) * 512]:
                             e.scalar_tensor_tensor(o, i, s, x, ALU.mult, ALU.add))
            b.barrier()

    _a_ctr = [0]

    def attn_call():
        nq = c.T // c.QUARTER
        cq = c.QUARTER // 128
        nm = f"a{_a_ctr[0]}"; _a_ctr[0] += 1
        with ExitStack() as astk:
            E = astk.enter_context
            a_xln = E(nc.sbuf_tensor(nm + "_xln", [128, c.D], F32))
            a_xlnT = E(nc.sbuf_tensor(nm + "_xlnT", [128, 8 * c.QUARTER], F32))
            dist_m = E(nc.sbuf_tensor(nm + "_dist", [128, 640], F32))
            b.dma("sy", dist_m[:], dist_in[:, :], ch="adist")
            b.wait("ve", "adist")
            a_qnT = E(nc.sbuf_tensor(nm + "_qnT", [128, 8 * c.QUARTER], F32))
            a_knr = E(nc.sbuf_tensor(nm + "_knr", [128, 8 * c.RING], F32))
            a_vr = E(nc.sbuf_tensor(nm + "_vr", [128, 8 * c.D], F32))
            a_w0 = E(nc.sbuf_tensor(nm + "_w0", [128, c.D], F32))
            a_w1 = E(nc.sbuf_tensor(nm + "_w1", [128, c.D], F32))
            a_pn00 = E(nc.sbuf_tensor(nm + "_pn00", [128, 640], F32))
            a_pn01 = E(nc.sbuf_tensor(nm + "_pn01", [128, 640], F32))
            a_pn10 = E(nc.sbuf_tensor(nm + "_pn10", [128, 640], F32))
            a_pn11 = E(nc.sbuf_tensor(nm + "_pn11", [128, 640], F32))
            a_pt0 = E(nc.sbuf_tensor(nm + "_pt0", [128, 640], F32))
            a_pt1 = E(nc.sbuf_tensor(nm + "_pt1", [128, 640], F32))
            a_aT4 = E(nc.sbuf_tensor(nm + "_aT4", [128, cq * 1024], F32))
            a_st = E(nc.sbuf_tensor(nm + "_st", [128, 8], F32))
            ap0 = E(nc.psum_tensor(nm + "_ap0", [128, 1024], F32))
            ap1 = E(nc.psum_tensor(nm + "_ap1", [128, 1024], F32))
            ap2 = E(nc.psum_tensor(nm + "_ap2", [128, 1024], F32))
            ap3 = E(nc.psum_tensor(nm + "_ap3", [128, 1024], F32))
            a_pn = [[a_pn00, a_pn01], [a_pn10, a_pn11]]
            a_pt = [a_pt0, a_pt1]
            a_w = [a_w0, a_w1]
            ap = [ap0, ap1, ap2, ap3]
            scps = [ap2, ap3]

            def seglist(kbeg, kend):
                segs = []
                kk = kbeg
                while kk < kend:
                    off = kk - kbeg
                    col = kk % c.RING
                    take = min(kend - kk, c.RING - col,
                               512 - (off % 512) if off % 512 else 512, 512)
                    segs.append((off, col, take))
                    kk += take
                return segs

            def emit_scores(h, j, q0q, par):
                q0 = q0q + j * 128
                kbeg = max(0, q0 - c.WINDOW)
                kend = q0 + 128
                nk = kend - kbeg
                segs = seglist(kbeg, kend)
                bank_first = {}
                bank_last = {}
                for s_ in segs:
                    bk = s_[0] // 512
                    bank_first.setdefault(bk, s_)
                    bank_last[bk] = s_
                hp = 64 * (h % 2)
                ht = h // 2
                pss = scps[par]
                for s_ in segs:
                    (off, col, take) = s_
                    lhs = a_qnT[hp:hp + 64,
                                ht * c.QUARTER + j * 128:ht * c.QUARTER + (j + 1) * 128]
                    rhs = a_knr[hp:hp + 64,
                                ht * c.RING + col:ht * c.RING + col + take]
                    b.op("te", lambda e, o=pss[:, off:off + take], l=lhs, r=rhs,
                         st=(bank_first[off // 512] == s_),
                         sp=(bank_last[off // 512] == s_):
                         e.matmul(o, l, r, start=st, stop=sp),
                         inc=(s_ == segs[-1]))
                return nk

            cap_sc = {}
            cap_stt = {}
            cap_exp = {}
            cap_pn = {}
            cap_ptsb = {}
            cap_ptT = {}
            cap_avp = {}
            cap_aTe = {}
            for qq in range(nq):
                q0q = qq * c.QUARTER
                for j in range(cq):
                    tc = qq * cq + j
                    b.wait("ve", "te")
                    layer_norm(tc, a_xln[:])
                    b.wait("te", "ve")
                    for g4 in range(2):
                        ps = scps[g4]
                        for ic4 in range(4):
                            ic = g4 * 4 + ic4
                            b.op("te", lambda e, o=ps[:, ic4 * 128:(ic4 + 1) * 128],
                                 s=a_xln[:, ic * 128:(ic + 1) * 128],
                                 st=(ic4 == 0), sp=(ic4 == 3):
                                 e.matmul(o, s, id32[:, :], start=st, stop=sp,
                                          is_transpose=True), inc=(ic4 == 3))
                    b.wait("se", "te")
                    for g4 in range(2):
                        ps = scps[g4]
                        b.op("se", lambda e,
                             o=a_xlnT[:].rearrange("p (ic t) -> p ic t", ic=8)
                             [:, g4 * 4:(g4 + 1) * 4, j * 128:(j + 1) * 128],
                             i=ps[:, 0:512].rearrange("p (ic t) -> p ic t", ic=4):
                             e.copy(o, i))

                cap_evj = {j_: (b.now("ve"), b.now("se")) for j_ in range(cq)}
                for proj in ("k", "v", "q"):
                    wT_src = {"q": wqT_d, "k": wkT_d, "v": wvT_d}[proj]
                    cap_w = {}
                    b.wait("sy", "te")
                    b.dma("sy", a_w[0][:], wT_src[0:128, :], ch="awl0")
                    cap_w[0] = b.now("awl0")
                    b.dma("sy", a_w[1][:], wT_src[128:256, :], ch="awl1")
                    cap_w[1] = b.now("awl1")
                    for ic in range(8):
                        b.wait("te", f"awl{ic % 2}", cap_w[ic])
                        for j in range(cq):
                            if ic == 0 and j in cap_evj:
                                b.wait("te", "ve", cap_evj[j][0])
                                b.wait("te", "se", cap_evj[j][1])
                            lhs = a_xlnT[:].rearrange("p (ic t) -> p ic t", ic=8)[
                                :, ic, j * 128:(j + 1) * 128]
                            for hh in range(2):
                                b.op("te", lambda e,
                                     o=ap[j][:, hh * 512:(hh + 1) * 512], l=lhs,
                                     r=a_w[ic % 2][:, hh * 512:(hh + 1) * 512],
                                     st=(ic == 0), sp=(ic == 7):
                                     e.matmul(o, l, r, start=st, stop=sp),
                                     inc=(j == cq - 1 and hh == 1))
                        if ic + 2 < 8:
                            b.wait("sy", "te")
                            b.dma("sy", a_w[ic % 2][:],
                                  wT_src[(ic + 2) * 128:(ic + 3) * 128, :],
                                  ch=f"awl{ic % 2}")
                            cap_w[ic + 2] = b.now(f"awl{ic % 2}")
                    cap_proj = b.now("te")
                    for j in range(cq):
                        if proj == "v":
                            vslot = (qq * cq + j) % 8
                            b.wait("ve", "te", cap_proj)
                            b.op("ve", lambda e,
                                 o=a_vr[:, vslot * c.D:(vslot + 1) * c.D],
                                 i=ap[j][:, :]: e.tensor_copy(o, i))
                            cap_evj[j] = (b.now("ve"), b.now("se"))
                            continue
                        b.wait("ve", "te", cap_proj)
                        b.op("ve", lambda e, o=ln["bn6"][:, 0:6], i=ap[j][:, 0:512]:
                             e.bn_stats(o, i))
                        b.op("ve", lambda e, o=ln["bn6"][:, 6:12], i=ap[j][:, 512:1024]:
                             e.bn_stats(o, i))
                        b.op("ve", lambda e, o=ln["mv"], i=ln["bn6"]: e.bn_aggr(o, i))
                        b.op("ve", lambda e, o=ln["m2"], a=ln["mv"][:, 0:1],
                             x=ln["mv"][:, 0:1]: e.tensor_tensor(o, a, x, ALU.mult))
                        b.op("ve", lambda e, o=ln["ex2"], a=ln["mv"][:, 1:2],
                             x=ln["m2"]: e.tensor_tensor(o, a, x, ALU.add))
                        b.wait("se", "ve")
                        b.op("se", lambda e, o=ln["std"], i=ln["ex2"]:
                             e.activation(o, i, ACTF.Sqrt, bias=c_eps_rms,
                                          scale=1.0))
                        b.wait("ve", "se")
                        b.op("ve", lambda e, o=ln["rstd"], i=ln["std"]:
                             e.reciprocal(o, i))
                        b.op("ve", lambda e, o=a_xln[:], i=ap[j][:, :], s=ln["rstd"]:
                             e.tensor_scalar(o, i, s, None, ALU.mult))
                        vcap = b.now("ve")
                        nw = qnw if proj == "q" else knw
                        for g4 in range(2):
                            b.wait("te", "ve", vcap if g4 == 0 else b.now("ve"))
                            for ic4 in range(4):
                                ic = g4 * 4 + ic4
                                b.op("te", lambda e,
                                     o=ap[j][:, ic4 * 128:(ic4 + 1) * 128],
                                     s=a_xln[:, ic * 128:(ic + 1) * 128],
                                     st=(ic4 == 0), sp=(ic4 == 3):
                                     e.matmul(o, s, id32[:, :], start=st, stop=sp,
                                              is_transpose=True), inc=(ic4 == 3))
                            tcap = b.now("te")
                            b.wait("ve", "te", tcap)
                            for ic4 in range(4):
                                ic = g4 * 4 + ic4
                                if proj == "q":
                                    dst = a_qnT[:].rearrange(
                                        "p (ic t) -> p ic t", ic=8)[
                                        :, ic, j * 128:(j + 1) * 128]
                                else:
                                    col = (q0q + j * 128) % c.RING
                                    dst = a_knr[:].rearrange(
                                        "p (ic t) -> p ic t", ic=8)[
                                        :, ic, col:col + 128]
                                b.op("ve", lambda e, o=dst,
                                     i=ap[j][:, ic4 * 128:(ic4 + 1) * 128],
                                     s=nw[:, ic:ic + 1]:
                                     e.tensor_scalar(o, i, s, None, ALU.mult))
                            cap_evj[j] = (b.now("ve"), b.now("se"))

                b.wait("te", "ve")

                def scores_ops(ph, j):
                    q0 = q0q + j * 128
                    kbeg_ = max(0, q0 - c.WINDOW)
                    kend = q0 + 128
                    segs = seglist(kbeg_, kend)
                    bank_first = {}
                    bank_last = {}
                    for s_ in segs:
                        bk = s_[0] // 512
                        bank_first.setdefault(bk, s_)
                        bank_last[bk] = s_
                    ops = []
                    for s_ in segs:
                        (off, col, take) = s_
                        for i01 in (0, 1):
                            hp = 64 * i01
                            ht = ph
                            lhs = a_qnT[hp:hp + 64,
                                        ht * c.QUARTER + j * 128:
                                        ht * c.QUARTER + (j + 1) * 128]
                            rhs = a_knr[hp:hp + 64,
                                        ht * c.RING + col:ht * c.RING + col + take]

                            def op_(o=scps[i01][:, off:off + take], l=lhs, r=rhs,
                                    st=(bank_first[off // 512] == s_),
                                    sp=(bank_last[off // 512] == s_),
                                    last=(s_ == segs[-1] and i01 == 1)):
                                b.op("te", lambda e, o=o, l=l, r=r, st=st, sp=sp:
                                     e.matmul(o, l, r, start=st, stop=sp), inc=last)
                            ops.append(op_)
                    return ops

                def pt_head(pend, i01):
                    (pj, pph, pnk, pnkc, pkbeg, ppar2) = pend
                    h = 2 * pph + i01
                    buf = a_pn[i01][ppar2]
                    pps = ap1 if i01 == 0 else ap0
                    if i01 == 0:
                        b.wait("te", "se", cap_ptsb.get((h - 2) % 16, 0))
                    else:
                        b.wait("te", "se", cap_aTe.get((pph - 1) % 8, 0))
                    b.wait("te", "ve", cap_pn[h])
                    for kc in range(pnkc):
                        st = (kc % 4 == 0)
                        sp = (kc % 4 == 3) or (kc == pnkc - 1)
                        b.op("te", lambda e,
                             o=pps[:, kc * 128:(kc + 1) * 128],
                             s=buf[:, kc * 128:(kc + 1) * 128],
                             st=st, sp=sp:
                             e.matmul(o, s, id32[:, :], start=st, stop=sp,
                                      is_transpose=True),
                             inc=(kc == pnkc - 1))
                    cap_ptT[h] = b.now("te")
                    b.wait("se", "te", cap_ptT[h])
                    b.op("se", lambda e, o=a_pt[i01][:, 0:pnk],
                         i=pps[:, 0:pnk]: e.copy(o, i))
                    cap_ptsb[h] = b.now("se")

                def av_pair(pend):
                    (pj, pph, pnk, pnkc, pkbeg, ppar2) = pend
                    h1_ = 2 * pph + 1
                    b.wait("te", "se", cap_aTe.get((pph - 1) % 8, 0))
                    b.wait("te", "se", cap_ptsb[h1_])
                    for i01 in (0, 1):
                        h = 2 * pph + i01
                        avc = i01 * 512
                        for kc in range(pnkc):
                            vslot = ((pkbeg + kc * 128) // 128) % 8
                            b.op("te", lambda e,
                                 o=ap0[i01 * 64:(i01 + 1) * 64, avc:avc + 128],
                                 l=a_vr[:, vslot * c.D + h * 64:
                                        vslot * c.D + (h + 1) * 64],
                                 r=a_pt[i01][:, kc * 128:(kc + 1) * 128],
                                 st=(kc == 0), sp=(kc == pnkc - 1),
                                 tp=(0, 64 * i01):
                                 e.matmul(o, l, r, start=st, stop=sp,
                                          tile_position=tp),
                                 inc=(kc == pnkc - 1 and i01 == 1))
                    cap_avp[pph] = b.now("te")
                    b.wait("se", "te", cap_avp[pph])
                    for i01 in (0, 1):
                        hp = 64 * i01
                        avc = i01 * 512
                        b.op("se", lambda e,
                             o=a_aT4[hp:hp + 64,
                                     pj * 1024 + pph * 128:pj * 1024 + (pph + 1) * 128],
                             i=ap0[hp:hp + 64, avc:avc + 128]: e.copy(o, i))
                    cap_aTe[pph] = b.now("se")

                pend = None
                for j in range(cq):
                    q0 = q0q + j * 128
                    kbeg = max(0, q0 - c.WINDOW)
                    nk = q0 + 128 - kbeg
                    nkc = nk // 128
                    coff = 640 - nk
                    if j == 0:
                        b.wait("te", "ve")
                        b.wait("te", "se")
                        for op_ in scores_ops(0, 0):
                            op_()
                        cap_sc[0] = cap_sc[1] = b.now("te")
                    for ph in range(c.H // 2):
                        gpq = j * 8 + ph
                        par2 = gpq % 2
                        h0, h1 = 2 * ph, 2 * ph + 1
                        for i01 in (0, 1):
                            h = 2 * ph + i01
                            buf = a_pn[i01][par2]
                            b.wait("ve", "te", cap_sc[h])
                            b.wait("ve", "se", cap_exp.get((h - 4) % 16, 0))
                            b.wait("ve", "te", cap_ptT.get((h - 4) % 16, 0))
                            b.op("ve", lambda e, o=buf[:, 0:nk],
                                 i=dist_m[:, coff:640], sl=8.0 * c.slopes[h],
                                 x=scps[i01][:, 0:nk]:
                                 e.scalar_tensor_tensor(o, i, sl, x,
                                                        ALU.mult, ALU.add))
                            cap_stt[h] = b.now("ve")
                            b.op("ve", lambda e, o=a_st[:, i01 * 4:i01 * 4 + 1],
                                 i=buf[:, 0:nk]:
                                 e.reduce_max(o, i, AX, negate=True))
                            b.op("ve", lambda e, o=a_st[:, i01 * 4 + 3:i01 * 4 + 4],
                                 i=a_st[:, i01 * 4:i01 * 4 + 1]:
                                 e.tensor_scalar(o, i, 0.125, None, ALU.mult))
                            b.wait("se", "ve")
                            b.op("se", lambda e, o=buf[:, 0:nk], i=buf[:, 0:nk],
                                 bi=a_st[:, i01 * 4 + 3:i01 * 4 + 4],
                                 a=a_st[:, i01 * 4 + 1:i01 * 4 + 2]:
                                 e.activation(o, i, ACTF.Exp, bias=bi, scale=0.125,
                                              accum_out=a))
                            cap_exp[h] = b.now("se")
                        for i01 in (0, 1):
                            h = 2 * ph + i01
                            buf = a_pn[i01][par2]
                            b.wait("ve", "se", cap_exp[h])
                            b.op("ve", lambda e, o=a_st[:, i01 * 4 + 2:i01 * 4 + 3],
                                 i=a_st[:, i01 * 4 + 1:i01 * 4 + 2]:
                                 e.reciprocal(o, i))
                            b.op("ve", lambda e, o=buf[:, 0:nk], i=buf[:, 0:nk],
                                 s=a_st[:, i01 * 4 + 2:i01 * 4 + 3]:
                                 e.tensor_scalar(o, i, s, None, ALU.mult))
                            cap_pn[h] = b.now("ve")
                        if ph + 1 < c.H // 2:
                            nxt = (ph + 1, j)
                        elif j + 1 < cq:
                            nxt = (0, j + 1)
                        else:
                            nxt = None
                        ops = scores_ops(*nxt) if nxt else []
                        if ops:
                            b.wait("te", "ve", cap_stt[h1])
                            for op_ in ops:
                                op_()
                            nh = 2 * nxt[0]
                            cap_sc[nh % 16] = cap_sc[(nh + 1) % 16] = b.now("te")
                        if pend is not None:
                            pt_head(pend, 0)
                            pt_head(pend, 1)
                            av_pair(pend)
                        pend = (j, ph, nk, nkc, kbeg, par2)
                pt_head(pend, 0)
                pt_head(pend, 1)
                av_pair(pend)
                pend = None

                cap_wo = {}
                b.wait("sy", "te")
                b.dma("sy", a_w[0][:], woT_d[0:128, :], ch="awl0")
                cap_wo[0] = b.now("awl0")
                b.dma("sy", a_w[1][:], woT_d[128:256, :], ch="awl1")
                cap_wo[1] = b.now("awl1")
                b.wait("te", "se")
                for jc in range(8):
                    b.wait("te", f"awl{jc % 2}", cap_wo[jc])
                    for j in range(cq):
                        for hh in range(2):
                            b.op("te", lambda e,
                                 o=ap[j][:, hh * 512:(hh + 1) * 512],
                                 l=a_aT4[:, j * 1024 + jc * 128:
                                         j * 1024 + (jc + 1) * 128],
                                 r=a_w[jc % 2][:, hh * 512:(hh + 1) * 512],
                                 st=(jc == 0), sp=(jc == 7):
                                 e.matmul(o, l, r, start=st, stop=sp),
                                 inc=(j == cq - 1 and hh == 1))
                    if jc + 2 < 8:
                        b.wait("sy", "te")
                        b.dma("sy", a_w[jc % 2][:],
                              woT_d[(jc + 2) * 128:(jc + 3) * 128, :],
                              ch=f"awl{jc % 2}")
                        cap_wo[jc + 2] = b.now(f"awl{jc % 2}")
                cap_womm = b.now("te")
                for j in range(cq):
                    tc = qq * cq + j
                    b.wait("ve", "te", cap_womm)
                    b.op("ve", lambda e, o=x_res[:, tc * c.D:(tc + 1) * c.D],
                         i=ap[j][:, :], x=x_res[:, tc * c.D:(tc + 1) * c.D]:
                         e.tensor_tensor(o, i, x, ALU.add))
            b.barrier()

    ffn_call()
    attn_call()
    ffn_call()

    b.barrier()
    for tc in range(n_tc):
        b.dma("ge", yout[tc * 128:(tc + 1) * 128, :],
              x_res[:, tc * c.D:(tc + 1) * c.D], ch="yout")
    sem, val = b.ch["yout"][0], b.now("yout")
    b.rec["ge"].append(lambda e, sem=sem, v=val: e.wait_ge(sem, v))

    with nc.Block() as block:
        @block.sync
        def _(eng):
            for fn in b.rec["sy"]:
                fn(eng)

        @block.tensor
        def _(eng):
            for fn in b.rec["te"]:
                fn(eng)

        @block.vector
        def _(eng):
            for fn in b.rec["ve"]:
                fn(eng)

        @block.scalar
        def _(eng):
            for fn in b.rec["se"]:
                fn(eng)

        @block.gpsimd
        def _(eng):
            for fn in b.rec["ge"]:
                fn(eng)

    stack.close()
    return nc


def make_consts(cfg: CFG):
    id32 = np.eye(128, dtype=np.float32)
    idbf = np.eye(128, dtype=ml_dtypes.bfloat16)
    r = np.arange(128)[:, None]
    ccol = np.arange(640)[None, :]
    dist = (ccol - 512 - r).astype(np.float32)
    valid = (dist <= 0) & (dist >= -cfg.WINDOW)
    dist_m = np.where(valid, dist, np.float32(-3e37)).astype(np.float32)
    return id32, idbf, dist_m


def make_inmaps(cfg, x, wq, wk, wv, wo, qnw, knw, wg, wu, wd):
    id32, idbf, dist_m = make_consts(cfg)
    f32 = lambda a: np.ascontiguousarray(np.asarray(a, dtype=np.float32))
    common = dict(wq=f32(wq), wk=f32(wk), wv=f32(wv), wo=f32(wo),
                  qnw=f32(qnw), knw=f32(knw), wg=f32(wg), wu=f32(wu), wd=f32(wd),
                  id32=id32, idbf=idbf, dist_m=dist_m)
    x = np.asarray(x, dtype=np.float32)
    return [dict(common, x=np.ascontiguousarray(x[i])) for i in range(x.shape[0])]


def kernel(L0_attn_x, L0_attn_wq, L0_attn_wk, L0_attn_wv, L0_attn_wo,
           L0_attn_q_norm_w, L0_attn_k_norm_w, L0_ffn_w_gate, L0_ffn_w_up,
           L0_ffn_w_down):
    cfg = CFG()
    nc = build_core(cfg)
    in_maps = make_inmaps(cfg, L0_attn_x, L0_attn_wq, L0_attn_wk, L0_attn_wv,
                          L0_attn_wo, L0_attn_q_norm_w, L0_attn_k_norm_w,
                          L0_ffn_w_gate, L0_ffn_w_up, L0_ffn_w_down)
    res = run_bass_kernel_spmd(nc, in_maps, core_ids=list(range(cfg.N_CORES)))
    return np.stack([r["y"] for r in res.results], axis=0)
